# version 13
# baseline (speedup 1.0000x reference)
import numpy as np
import concourse.bass as bass
import concourse.mybir as mybir

F32 = mybir.dt.float32
BF16 = mybir.dt.bfloat16
AF = mybir.ActivationFunctionType
ALU = mybir.AluOpType
AX = mybir.AxisListType
MAGIC = 12582912.0
QMAX = 32767.0
LN8 = 2.0794415416798357
D = 1024
DCH = 8
NP = 8
ICH = 24
LN_EPS = 1e-5
ATT_EPS = 1e-6


class Sem:
    def __init__(self, nc, es, name):
        self.h = es.enter_context(nc.semaphore(name))
        self.n = 0

    def inc(self, instr, k=1):
        instr.then_inc(self.h, k)
        self.n += k
        return instr


def build(T, n_cores=8, race=True):
    import contextlib
    TC = T // 128
    TT = T // 512
    assert T % 512 == 0

    nc = bass.Bass("TRN2", target_bir_lowering=False, debug=False,
                   detect_race_conditions=race)

    x = nc.dram_tensor("x", [T, D], F32, kind="ExternalInput").ap()
    qkv_wT = nc.dram_tensor("qkv_wT", [D, 3 * D], F32, kind="ExternalInput").ap()
    qkv_b = nc.dram_tensor("qkv_b", [3 * D], F32, kind="ExternalInput").ap()
    proj_wT = nc.dram_tensor("proj_wT", [D, D], F32, kind="ExternalInput").ap()
    proj_b = nc.dram_tensor("proj_b", [D], F32, kind="ExternalInput").ap()
    pmatT = nc.dram_tensor("pmatT", [64, 64], F32, kind="ExternalInput").ap()
    qkbh = nc.dram_tensor("qkbh", [128, 16], F32, kind="ExternalInput").ap()
    w_gate = nc.dram_tensor("w_gate", [3072, D], F32, kind="ExternalInput").ap()
    w_up = nc.dram_tensor("w_up", [3072, D], F32, kind="ExternalInput").ap()
    w_down = nc.dram_tensor("w_down", [D, 3072], F32, kind="ExternalInput").ap()
    selc = nc.dram_tensor("selc", [16, NP * 128], F32, kind="ExternalInput").ap()
    y = nc.dram_tensor("y", [T, D], F32, kind="ExternalOutput").ap()

    sp_qp = nc.dram_tensor("sp_qp", [128, NP, T], BF16).ap()
    sp_oe = nc.dram_tensor("sp_oe", [128, DCH, T], BF16).ap()
    sp_g = nc.dram_tensor("sp_g", [ICH, 128, T], BF16).ap()
    sp_u = nc.dram_tensor("sp_u", [ICH, 128, T], BF16).ap()
    sp_g2 = nc.dram_tensor("sp_g2", [ICH, 128, T], F32).ap()
    sp_or = nc.dram_tensor("sp_or", [ICH, 128, T], F32).ap()

    cc1_in = nc.dram_tensor("cc1_in", [128], F32).ap()
    cc1_out = nc.dram_tensor("cc1_out", [128], F32).ap()
    cc2_in = nc.dram_tensor("cc2_in", [128, 1032], F32).ap()
    cc2_out = nc.dram_tensor("cc2_out", [128, 1032], F32).ap()
    cc3_in = nc.dram_tensor("cc3_in", [128], F32).ap()
    cc3_out = nc.dram_tensor("cc3_out", [128], F32).ap()
    cc4_in = nc.dram_tensor("cc4_in", [256], F32).ap()
    cc4_out = nc.dram_tensor("cc4_out", [256], F32).ap()
    cc5_in = nc.dram_tensor("cc5_in", [128], F32).ap()
    cc5_out = nc.dram_tensor("cc5_out", [128], F32).ap()
    scr1 = nc.dram_tensor("scr1", [2], F32).ap()
    scr3 = nc.dram_tensor("scr3", [2], F32).ap()
    scr4 = nc.dram_tensor("scr4", [8], F32).ap()
    scr5 = nc.dram_tensor("scr5", [2], F32).ap()

    es = contextlib.ExitStack()
    with es:
        projBD = es.enter_context(nc.sbuf_tensor("projBD", [128, 128], BF16))
        ones_r = es.enter_context(nc.sbuf_tensor("ones_r", [1, 128], F32))
        ones_cb = es.enter_context(nc.sbuf_tensor("ones_cb", [128, 1], BF16))
        qkb = es.enter_context(nc.sbuf_tensor("qkb", [128, 16], F32))
        qvb = es.enter_context(nc.sbuf_tensor("qvb", [1, D], F32))
        prb = es.enter_context(nc.sbuf_tensor("prb", [1, D], F32))
        rstd_s = es.enter_context(nc.sbuf_tensor("rstd_s", [128, TC], F32))
        nmr_s = es.enter_context(nc.sbuf_tensor("nmr_s", [128, TC], F32))
        xscb = es.enter_context(nc.sbuf_tensor("xscb", [128, TC], F32))
        xbib = es.enter_context(nc.sbuf_tensor("xbib", [128, TC], F32))
        m1arr = es.enter_context(nc.sbuf_tensor("m1arr", [128, TC], F32))
        m3arr = es.enter_context(nc.sbuf_tensor("m3arr", [128, ICH], F32))
        m2arr = es.enter_context(nc.sbuf_tensor("m2arr", [128, ICH], F32))
        m4arr = es.enter_context(nc.sbuf_tensor("m4arr", [128, 2 * ICH], F32))
        m5arr = es.enter_context(nc.sbuf_tensor("m5arr", [128, ICH], F32))
        sqt = es.enter_context(nc.sbuf_tensor("sqt", [128, 4, 1], F32))
        mv2 = es.enter_context(nc.sbuf_tensor("mv2", [128, 4, 2], F32))
        bnb = es.enter_context(nc.sbuf_tensor("bnb", [128, 2, 2, 6], F32))
        maxg = es.enter_context(nc.sbuf_tensor("maxg", [128, 2, 8], F32))
        sc1 = es.enter_context(nc.sbuf_tensor("sc1", [128, 2], F32))
        sc3 = es.enter_context(nc.sbuf_tensor("sc3", [128, 2], F32))
        sc24 = es.enter_context(nc.sbuf_tensor("sc24", [128, 8], F32))
        sc5 = es.enter_context(nc.sbuf_tensor("sc5", [128, 2], F32))
        mrow = es.enter_context(nc.sbuf_tensor("mrow", [1, 2, 128], F32))
        msc = es.enter_context(nc.sbuf_tensor("msc", [1, 8], F32))
        pm24 = es.enter_context(nc.sbuf_tensor("pm24", [128, 2], F32))
        cstM = es.enter_context(nc.sbuf_tensor("cstM", [128, 1], F32))
        cstMn = es.enter_context(nc.sbuf_tensor("cstMn", [128, 1], F32))
        cstE = es.enter_context(nc.sbuf_tensor("cstE", [128, 1], F32))
        cstL8 = es.enter_context(nc.sbuf_tensor("cstL8", [128, 1], F32))
        epsc = es.enter_context(nc.sbuf_tensor("epsc", [128, 1], F32))
        wsc = es.enter_context(nc.sbuf_tensor("wsc", [128, 2, 32], F32))
        wrc = es.enter_context(nc.sbuf_tensor("wrc", [128, 2, 32], F32))

        SB_TOP = nc.sbuf_top
        def man(name, shape, dtype, off):
            return nc.alloc_sbuf_tensor_at(name, shape, dtype, offset=off)

        SZ_BIG = DCH * T * 2
        assert SZ_BIG == ICH * D * 2 + DCH * D * 2
        R_BIG = SB_TOP - SZ_BIG
        bigA = man("bigA", [128, DCH, T], BF16, R_BIG)
        downT = man("downT", [128, ICH, D], BF16, R_BIG)
        projT = man("projT", [128, DCH, D], BF16, R_BIG + ICH * D * 2)

        SZ_Q = DCH * 3072 * 2
        R_Q = R_BIG - SZ_Q
        qkvT = man("qkvT", [128, DCH, 3072], BF16, R_Q)
        fst = man("fst", [128, 2, T], BF16, R_Q)
        gst = man("gst", [128, 2, T], BF16, R_Q + 16384)
        wnat = man("wnat", [128, 2, D], F32, R_Q + 32768)
        wbq = man("wbq", [128, 2, D], BF16, R_Q + 40960)
        gT = man("gT", [128, 2, DCH, 128], BF16, R_Q + 45056)
        orst = man("orst", [128, ICH, 512], F32, R_Q)
        pjst = man("pjst", [128, 2, D], F32, R_Q)

        SZ_W = 8192 + 8192 + 4096 + 16384 + 8192 + 4096 + 4096 + 4096 + 2048 + 16384 + 4224
        R_W = R_Q - SZ_W
        assert R_W >= nc.sbuf_base, (R_W, nc.sbuf_base)
        wst = man("wst", [128, 2, D], F32, R_W)
        xf = man("xf", [128, 2, D], F32, R_W + 8192)
        xtm = man("xtm", [128, 2, D], BF16, R_W + 16384)
        qfm = man("qfm", [128, 16, 512], BF16, R_W + 20480)
        qptm = man("qptm", [128, 4, D], BF16, R_W + 36864)
        kptm = man("kptm", [128, 2, D], BF16, R_W + 45056)
        vtm = man("vtm", [128, 2, D], BF16, R_W + 49152)
        esub = man("esub", [128, 2, 512], F32, R_W + 53248)
        estg = man("estg", [128, 2, 512], BF16, R_W + 57344)
        qproll = man("qproll", [128, 2, NP, 512], BF16, R_W + 59392)
        kv_acc = man("kv_acc", [128, 1032], F32, R_W + 75776)
        assert 75776 + 4224 <= SZ_W
        kpt2 = man("kpt2", [128, 2, D], BF16, R_W)
        vt2 = man("vt2", [128, 2, D], BF16, R_W + 4096)
        xnf = man("xnf", [128, 2, D], F32, R_W)
        kvbd = man("kvbd", [128, NP, 128], BF16, R_W + 36864)
        normW = man("normW", [128, NP, 16], BF16, R_W + 38912)
        qsc = man("qsc", [128, 512], BF16, R_W + 39168)
        nrm_sb = man("nrm_sb", [16, 512], F32, R_W + 40192)
        rec_sb = man("rec_sb", [16, 512], F32, R_W + 42240)
        selP = man("selP", [16, NP, 128], F32, R_W + 44288)
        assert 44288 + 4096 <= 49152
        t1s = man("t1s", [128, T], F32, R_W + 8192)
        g1b = man("g1b", [128, T], F32, R_W + 24576)
        t1h = man("t1h", [128, 2, T // 2], F32, R_W + 8192)
        g1h = man("g1h", [128, 2, T // 2], F32, R_W + 24576)
        iva_u = man("iva_u", [128, 2, T], F32, R_W + 8192)
        g2r = man("g2r", [128, 2, T], F32, R_W + 40960)
        assert 40960 + 32768 <= SZ_W
        oTt = man("oTt", [128, 2, ICH, 512], BF16, R_W)
        oest = man("oest", [128, 2, DCH, 128], BF16, R_W + 49152)
        xrs = man("xrs", [128, 2, D], F32, R_W + 53248)
        osb = man("osb", [128, 2, D], F32, R_W + 61440)
        assert 61440 + 8192 <= SZ_W

        names = ("s_pr s_prd s_wq s_wqc s_xa0 s_xa1 s_bn s_ln s_le s_st s_ap s_mx "
                 "s_tx0 s_tx1 s_qm s_qe s_vm s_ve s_cm s_cr s_ee s_ep "
                 "s_tq0 s_tq1 s_qw0 s_qw1 s_kv s_kve "
                 "s_ccd s_cc s_bar s_bars s_gb "
                 "s_a2 s_v2 s_qr0 s_qr1 s_dm s_dv s_de s_ow0 s_ow1 "
                 "s_wl0 s_wl1 s_dqv s_dqs s_dq2 s_wt0 s_wt1 s_fm s_fe s_fr "
                 "s_fw0 s_fw1 s_r30 s_r31 s_3s s_3v s_gw0 s_gw1 "
                 "s_g2r0 s_g2r1 s_4s s_4v s_orw0 s_orw1 s_pj s_pjl "
                 "s_rf0 s_rf1 s_ord s_5s s_5v s_tf s_vf s_out0 s_out1").split()
        S = {nm: Sem(nc, es, nm) for nm in names}
        (s_pr, s_prd, s_wq, s_wqc, s_xa0, s_xa1, s_bn, s_ln, s_le, s_st, s_ap, s_mx,
         s_tx0, s_tx1, s_qm, s_qe, s_vm, s_ve, s_cm, s_cr, s_ee, s_ep,
         s_tq0, s_tq1, s_qw0, s_qw1, s_kv, s_kve,
         s_ccd, s_cc, s_bar, s_bars, s_gb,
         s_a2, s_v2, s_qr0, s_qr1, s_dm, s_dv, s_de, s_ow0, s_ow1,
         s_wl0, s_wl1, s_dqv, s_dqs, s_dq2, s_wt0, s_wt1, s_fm, s_fe, s_fr,
         s_fw0, s_fw1, s_r30, s_r31, s_3s, s_3v, s_gw0, s_gw1,
         s_g2r0, s_g2r1, s_4s, s_4v, s_orw0, s_orw1, s_pj, s_pjl,
         s_rf0, s_rf1, s_ord, s_5s, s_5v, s_tf, s_vf, s_out0, s_out1) = (
            S[nm] for nm in names)
        sXA = [s_xa0, s_xa1]
        sTX = [s_tx0, s_tx1]
        sTQ = [s_tq0, s_tq1]
        sQW = [s_qw0, s_qw1]
        sQR = [s_qr0, s_qr1]
        sOW = [s_ow0, s_ow1]
        sWL = [s_wl0, s_wl1]
        sWT = [s_wt0, s_wt1]
        sFW = [s_fw0, s_fw1]
        sR3 = [s_r30, s_r31]
        sG2R = [s_g2r0, s_g2r1]
        sGW = [s_gw0, s_gw1]
        sORW = [s_orw0, s_orw1]
        sRF = [s_rf0, s_rf1]
        sOUT = [s_out0, s_out1]

        pb = [es.enter_context(nc.psum_tensor(f"pb{i}", [128, 512], F32))
              for i in range(8)]

        blk = es.enter_context(nc.Block())

        @blk.gpsimd
        def _(g):
            g.memset(ones_r[:], 1.0)
            g.memset(ones_cb[:], 1.0)
            g.memset(projBD[:], 0.0)
            g.memset(cstM[:], MAGIC)
            g.memset(cstMn[:], -MAGIC)
            g.memset(cstE[:], LN_EPS)
            g.memset(cstL8[:], -LN8)
            s_pr.inc(g.memset(epsc[:], ATT_EPS))
            s_prd.inc(g.dma_start(out=esub[0:64, 0, 0:64], in_=pmatT), 16)
            s_prd.inc(g.dma_start(out=qkb[:], in_=qkbh), 16)
            s_prd.inc(g.dma_start(out=qvb[:], in_=qkv_b[2048:3072].unsqueeze(0)), 16)
            s_prd.inc(g.dma_start(out=prb[:], in_=proj_b.unsqueeze(0)), 16)
            for c in range(DCH):
                for t3 in range(3):
                    i = 3 * c + t3
                    if i >= 2:
                        g.wait_ge(s_wqc.h, i - 1)
                    s_wq.inc(g.dma_start(
                        out=wst[:, i % 2, :],
                        in_=qkv_wT[c * 128:(c + 1) * 128, t3 * 1024:(t3 + 1) * 1024]), 16)

        @blk.scalar
        def _(a):
            for c in range(DCH):
                for t3 in range(3):
                    i = 3 * c + t3
                    a.wait_ge(s_wq.h, 32 * (i // 2 + 1))
                    s_wqc.inc(a.copy(qkvT[:, c, t3 * 1024:(t3 + 1) * 1024],
                                     wst[:, i % 2, :]))
            s_wqc.inc(a.drain())

        @blk.vector
        def _(v):
            v.wait_ge(s_pr.h, 1)
            v.wait_ge(s_prd.h, 64)
            v.tensor_scalar(projBD[0:64, 0:64], esub[0:64, 0, 0:64], 0.125, None,
                            ALU.mult)
            v.tensor_scalar(projBD[64:128, 64:128], esub[0:64, 0, 0:64],
                            0.125, None, ALU.mult)
            s_cr.inc(v.drain())
        CR_P = s_cr.n

        @blk.sync
        def _(yq):
            for g in range(2):
                sXA[g % 2].inc(yq.dma_start(out=xf[:, g % 2, :],
                                            in_=x[g * 128:(g + 1) * 128, :]), 16)

        def emitA_chunk(g):
            @blk.sync
            def _(yq, g=g):
                ld = g + 2
                if ld < TC:
                    yq.wait_ge(s_ap.h, g + 1)
                    sXA[ld % 2].inc(yq.dma_start(out=xf[:, ld % 2, :],
                                                 in_=x[ld * 128:(ld + 1) * 128, :]), 16)

            @blk.vector
            def _(v, g=g):
                v.wait_ge(sXA[g % 2].h, 16 * (g // 2 + 1))
                if g >= 4:
                    v.wait_ge(s_ln.h, g - 3)
                v.bn_stats(bnb[:, g % 2, 0, :], xf[:, g % 2, 0:512])
                v.bn_stats(bnb[:, g % 2, 1, :], xf[:, g % 2, 512:1024])
                v.drain()
                s_bn.inc(v.bn_aggr(mv2[:, g % 4, :], bnb[:, g % 2, :, :]))
                if g >= 1:
                    gp = g - 1
                    v.wait_ge(s_le.h, gp + 1)
                    s_st.inc(v.scalar_tensor_tensor(
                        nmr_s[:, gp:gp + 1], mv2[:, gp % 4, 0:1], -1.0,
                        rstd_s[:, gp:gp + 1], ALU.mult, ALU.mult))
                    if gp >= 2:
                        v.wait_ge(sTX[gp % 2].h, 128 * ((gp - 2) // 2 + 1))
                    s_ap.inc(v.tensor_scalar(xtm[:, gp % 2, :], xf[:, gp % 2, :],
                                             mv2[:, gp % 4, 0:1],
                                             rstd_s[:, gp:gp + 1],
                                             ALU.subtract, ALU.mult))
                if g >= 2:
                    s_mx.inc(v.tensor_reduce(m1arr[:, g - 2:g - 1],
                                             xtm[:, g % 2, :], AX.X, ALU.max,
                                             apply_absolute_value=True))

            @blk.scalar
            def _(a, g=g):
                a.wait_ge(s_bn.h, g + 1)
                if g >= 4:
                    a.wait_ge(s_le.h, g - 3)
                s_ln.inc(a.activation(sqt[:, g % 4, :], mv2[:, g % 4, 1:2],
                                      AF.Ln, bias=cstE[:]))
                if g >= 1:
                    j = g - 1
                    a.wait_ge(s_ln.h, j + 1)
                    s_le.inc(a.activation(rstd_s[:, j:j + 1], sqt[:, j % 4, :],
                                          AF.Exp, scale=-0.5))

            @blk.sync
            def _(yq, g=g):
                if g >= 1:
                    gp = g - 1
                    yq.wait_ge(sTX[gp % 2].h, sTX[gp % 2].n)
                    yq.wait_ge(s_ap.h, gp + 1)
                    for k in range(DCH):
                        sTX[gp % 2].inc(yq.dma_start_transpose(
                            bigA[:, k, gp * 128:(gp + 1) * 128],
                            xtm[:, gp % 2, k * 128:(k + 1) * 128]), 16)

        def emitA(tt0):
            for c in range(4):
                emitA_chunk(tt0 * 4 + c)

        def emitA_tail():
            @blk.scalar
            def _(a):
                j = TC - 1
                a.wait_ge(s_ln.h, j + 1)
                s_le.inc(a.activation(rstd_s[:, j:j + 1], sqt[:, j % 4, :],
                                      AF.Exp, scale=-0.5))

            @blk.vector
            def _(v):
                gp = TC - 1
                v.drain()
                v.wait_ge(s_le.h, gp + 1)
                s_st.inc(v.scalar_tensor_tensor(
                    nmr_s[:, gp:gp + 1], mv2[:, gp % 4, 0:1], -1.0,
                    rstd_s[:, gp:gp + 1], ALU.mult, ALU.mult))
                v.wait_ge(sTX[gp % 2].h, 128 * ((gp - 2) // 2 + 1))
                s_ap.inc(v.tensor_scalar(xtm[:, gp % 2, :], xf[:, gp % 2, :],
                                         mv2[:, gp % 4, 0:1], rstd_s[:, gp:gp + 1],
                                         ALU.subtract, ALU.mult))
                s_mx.inc(v.tensor_reduce(m1arr[:, TC - 2:TC - 1],
                                         xtm[:, TC % 2, :], AX.X, ALU.max,
                                         apply_absolute_value=True))

            @blk.sync
            def _(yq):
                gp = TC - 1
                yq.wait_ge(sTX[gp % 2].h, sTX[gp % 2].n)
                yq.wait_ge(s_ap.h, gp + 1)
                for k in range(DCH):
                    sTX[gp % 2].inc(yq.dma_start_transpose(
                        bigA[:, k, gp * 128:(gp + 1) * 128],
                        xtm[:, gp % 2, k * 128:(k + 1) * 128]), 16)

            @blk.vector
            def _(v):
                v.drain()
                s_mx.inc(v.tensor_reduce(m1arr[:, TC - 1:TC],
                                         xtm[:, (TC - 1) % 2, :], AX.X, ALU.max,
                                         apply_absolute_value=True))

        def tq_done(tt, c):
            return 16 * 32 * (tt // 2) + 16 * 8 * (c + 1)

        def kp_ap(g):
            return (kptm if (g % 4) < 2 else kpt2)[:, g % 2, :]

        def vt_ap(g):
            return (vtm if (g % 4) < 2 else vt2)[:, g % 2, :]

        def emitB(tt):
            @blk.tensor
            def _(e, tt=tt):
                if tt == 0:
                    e.wait_ge(s_wqc.h, 25)
                    e.wait_ge(s_cr.h, CR_P)
                e.wait_ge(sTX[0].h, 128 * (2 * tt + 2))
                e.wait_ge(sTX[1].h, 128 * (2 * tt + 2))
                for jc in range(16):
                    bank = pb[jc % 2]
                    if tt * 16 + jc >= 2:
                        e.wait_ge(s_qe.h, tt * 16 + jc - 1)
                    for k in range(DCH):
                        mm = e.matmul(bank[:], qkvT[:, k, jc * 128:(jc + 1) * 128],
                                      bigA[:, k, tt * 512:(tt + 1) * 512],
                                      start=(k == 0), stop=(k == DCH - 1))
                    s_qm.inc(mm)

            @blk.vector
            def _(v, tt=tt):
                for jc in range(16):
                    v.wait_ge(s_qm.h, tt * 16 + jc + 1)
                    if tt >= 1 and jc == 0:
                        v.wait_ge(s_cm.h, 16 * tt)
                    s_qe.inc(v.tensor_scalar(qfm[:, jc, :], pb[jc % 2][:],
                                             qkb[:, jc:jc + 1], None, ALU.add))

            for c in range(4):
                g = tt * 4 + c

                @blk.tensor
                def _(e, tt=tt, c=c, g=g):
                    for half in range(2):
                        vidx = 2 * g + half
                        if vidx >= 1:
                            e.wait_ge(s_ve.h, vidx)
                        for k in range(DCH):
                            e.matmul(pb[2][:], bigA[:, k, g * 128:(g + 1) * 128],
                                     qkvT[:, k, 2048 + half * 512:2048 + (half + 1) * 512],
                                     start=(k == 0), stop=False)
                        mm = e.matmul(pb[2][:], ones_r[0:1, 0:128],
                                      qvb[0:1, half * 512:(half + 1) * 512],
                                      start=False, stop=True)
                        s_vm.inc(mm)
                    if c == 0:
                        e.wait_ge(s_qe.h, tt * 16 + 16)
                    for grp in range(4):
                        bank = pb[3 + grp % 2]
                        gidx = 4 * g + grp
                        if gidx >= 2:
                            e.wait_ge(s_cr.h, CR_P + gidx - 1)
                        for p4 in range(4):
                            p = (grp % 2) * 4 + p4
                            jj = (8 if grp >= 2 else 0) + p
                            mm = e.matmul(bank[:, p4 * 128:(p4 + 1) * 128],
                                          qfm[:, jj, c * 128:(c + 1) * 128],
                                          projBD[:], start=True, stop=True)
                        s_cm.inc(mm)

                @blk.scalar
                def _(a, tt=tt, c=c, g=g):
                    for half in range(2):
                        vidx = 2 * g + half
                        a.wait_ge(s_vm.h, vidx + 1)
                        if g >= 4 and half == 0:
                            a.wait_ge(s_kv.h, g - 3)
                        s_ve.inc(a.activation(vt_ap(g)[:, half * 512:(half + 1) * 512],
                                              pb[2][:], AF.Identity))
                    for grp in range(4):
                        gidx = 4 * g + grp
                        a.wait_ge(s_cr.h, CR_P + gidx + 1)
                        if gidx >= 2:
                            a.wait_ge(s_ep.h, gidx - 1)
                        s_ee.inc(a.activation(estg[:, grp % 2, :], esub[:, grp % 2, :],
                                              AF.Exp, bias=cstL8[:]))

                @blk.vector
                def _(v, tt=tt, c=c, g=g):
                    for grp in range(4):
                        bank = pb[3 + grp % 2]
                        gidx = 4 * g + grp
                        v.wait_ge(s_cm.h, gidx + 1)
                        bv = bank[:].rearrange("p (h m) -> p h m", m=64)
                        v.tensor_reduce(maxg[:, grp % 2, :].unsqueeze(2), bv,
                                        AX.X, ALU.max)
                        v.drain()
                        if gidx >= 2:
                            v.wait_ge(s_ee.h, gidx - 1)
                        s_cr.inc(v.tensor_tensor(
                            esub[:, grp % 2, :].rearrange("p (h m) -> p h m", m=64), bv,
                            maxg[:, grp % 2, :].unsqueeze(2).broadcast_to([128, 8, 64]),
                            ALU.subtract))
                        v.wait_ge(s_ee.h, gidx + 1)
                        half = grp % 2
                        if grp < 2:
                            if tt >= 1:
                                v.wait_ge(sTQ[(tt - 1) % 2].h, tq_done(tt - 1, c))
                            dst = qptm[:, g % 4, half * 512:(half + 1) * 512]
                        else:
                            if g >= 4:
                                v.wait_ge(s_kv.h, g - 3)
                            dst = kp_ap(g)[:, half * 512:(half + 1) * 512]
                        s_ep.inc(v.tensor_scalar(dst, estg[:, grp % 2, :], epsc[:],
                                                 None, ALU.add))

                @blk.sync
                def _(yq, tt=tt, c=c, g=g):
                    yq.wait_ge(sTQ[tt % 2].h, sTQ[tt % 2].n)
                    yq.wait_ge(s_ep.h, 4 * g + 2)
                    if tt >= 2 and c == 0:
                        yq.wait_ge(sQW[tt % 2].h, 16 * ((tt - 2) // 2 + 1))
                    for p in range(NP):
                        sTQ[tt % 2].inc(yq.dma_start_transpose(
                            qproll[:, tt % 2, p, c * 128:(c + 1) * 128],
                            qptm[:, g % 4, p * 128:(p + 1) * 128]), 16)

            @blk.tensor
            def _(e, tt=tt):
                for c in range(4):
                    g = tt * 4 + c
                    e.wait_ge(s_ep.h, 4 * g + 4)
                    e.wait_ge(s_ve.h, 2 * g + 2)
                    for p in range(NP):
                        bank = pb[5 + p // 4]
                        e.matmul(bank[:, (p % 4) * 128:(p % 4 + 1) * 128],
                                 kp_ap(g)[:, p * 128:(p + 1) * 128],
                                 vt_ap(g)[:, p * 128:(p + 1) * 128],
                                 start=(g == 0), stop=(g == TC - 1),
                                 skip_group_check=True)
                    for p in range(NP):
                        mm = e.matmul(pb[7][:, p:p + 1],
                                      kp_ap(g)[:, p * 128:(p + 1) * 128],
                                      ones_cb[:], start=(g == 0), stop=(g == TC - 1),
                                      skip_group_check=True)
                    s_kv.inc(mm)

            @blk.gpsimd
            def _(g_, tt=tt):
                g_.wait_ge(sTQ[tt % 2].h, tq_done(tt, 3))
                sQW[tt % 2].inc(g_.dma_start(out=sp_qp[:, :, tt * 512:(tt + 1) * 512],
                                             in_=qproll[:, tt % 2, :, :]), 16)

        for tt in range(TT):
            emitA(tt)
            if tt > 0:
                emitB(tt - 1)
        emitA_tail()
        emitB(TT - 1)
        TQ_TOT = [sTQ[0].n, sTQ[1].n]

        @blk.scalar
        def _(a):
            a.wait_ge(s_kv.h, TC)
            a.copy(kv_acc[:, 0:512], pb[5][:])
            a.copy(kv_acc[:, 512:1024], pb[6][:])
            a.copy(kv_acc[:, 1024:1032], pb[7][:, 0:8])
            s_kve.inc(a.drain())

        @blk.vector
        def _(v):
            v.wait_ge(s_mx.h, TC)
            v.tensor_reduce(pm24[:, 0:1], m1arr[:], AX.X, ALU.max)
            s_bar.inc(v.drain())
        B1 = s_bar.n

        @blk.gpsimd
        def _(g):
            g.wait_ge(s_bar.h, B1)
            s_ccd.inc(g.dma_start(out=cc1_in.unsqueeze(1), in_=pm24[:, 0:1]), 16)
            g.wait_ge(s_ccd.h, s_ccd.n)
            s_cc.inc(g.collective_compute(
                "AllReduce", ALU.max, replica_groups=[list(range(n_cores))],
                ins=[cc1_in.opt()], outs=[cc1_out.opt()]))
            g.wait_ge(s_cc.h, s_cc.n)
            s_ccd.inc(g.dma_start(out=mrow[:, 0, :], in_=cc1_out.unsqueeze(0)), 16)
        CC1_D = s_ccd.n

        @blk.vector
        def _(v):
            v.wait_ge(s_ccd.h, CC1_D)
            v.tensor_reduce(msc[:, 0:1].unsqueeze(2), mrow[:, 0:1, :], AX.X, ALU.max)
            v.drain()
            v.tensor_scalar(msc[:, 0:1], msc[:, 0:1], 1.0 / QMAX, 1e-12,
                            ALU.mult, ALU.max)
            v.drain()
            v.reciprocal(msc[:, 1:2], msc[:, 0:1])
            s_bar.inc(v.drain())
        B1b = s_bar.n

        @blk.sync
        def _(yq):
            yq.wait_ge(s_bar.h, B1b)
            s_bars.inc(yq.dma_start(out=scr1.unsqueeze(0), in_=msc[0:1, 0:2]), 16)
        B1c = s_bars.n

        @blk.gpsimd
        def _(g):
            g.wait_ge(s_bars.h, B1c)
            s_gb.inc(g.dma_start(out=sc1[:].unsqueeze(1),
                                 in_=scr1.unsqueeze(0).partition_broadcast(128)), 16)
        GB1 = s_gb.n

        @blk.vector
        def _(v):
            v.wait_ge(s_gb.h, GB1)
            v.tensor_scalar(xscb[:], rstd_s[:], sc1[:, 1:2], None, ALU.mult)
            v.tensor_scalar(xbib[:], nmr_s[:], sc1[:, 1:2], MAGIC,
                            ALU.mult, ALU.add)
            s_st.inc(v.drain())
        XSC_ST = s_st.n

        groups = [list(range(n_cores // 2)), list(range(n_cores // 2, n_cores))]

        @blk.gpsimd
        def _(g):
            g.wait_ge(s_ccd.h, s_ccd.n)
            g.wait_ge(s_kve.h, 1)
            s_ccd.inc(g.dma_start(out=cc2_in, in_=kv_acc[:]), 16)
            g.wait_ge(s_ccd.h, s_ccd.n)
            s_cc.inc(g.collective_compute(
                "AllReduce", ALU.add, replica_groups=groups,
                ins=[cc2_in.opt()], outs=[cc2_out.opt()]))
            g.wait_ge(s_cc.h, s_cc.n)
            s_ccd.inc(g.dma_start(out=kv_acc[:], in_=cc2_out), 16)
        CC2_D = s_ccd.n

        @blk.gpsimd
        def _(g):
            g.wait_ge(s_gb.h, s_gb.n)
            g.wait_ge(s_kv.h, TC)
            g.wait_ge(sTQ[0].h, TQ_TOT[0])
            g.wait_ge(sTQ[1].h, TQ_TOT[1])
            s_gb.inc(g.dma_start(out=selP[:].rearrange("q p c -> q (p c)"), in_=selc), 16)
        SEL_D = s_gb.n

        for g in range(TC):
            @blk.sync
            def _(yq, g=g):
                if g == 0:
                    yq.wait_ge(s_ap.h, TC)
                yq.wait_ge(s_a2.h, max(0, g - 1))
                sXA[g % 2].inc(yq.dma_start(out=xf[:, g % 2, :],
                                            in_=x[g * 128:(g + 1) * 128, :]), 16)

            @blk.scalar
            def _(a, g=g):
                a.wait_ge(sXA[g % 2].h, 16 * (TC // 2 + g // 2 + 1))
                if g == 0:
                    a.wait_ge(s_st.h, XSC_ST)
                    a.wait_ge(s_wqc.h, 24)
                    a.wait_ge(s_kv.h, TC)
                if g >= 2:
                    a.wait_ge(s_v2.h, g - 1)
                s_a2.inc(a.activation(xnf[:, g % 2, :], xf[:, g % 2, :],
                                      AF.Identity, bias=xbib[:, g:g + 1],
                                      scale=xscb[:, g:g + 1]))

            @blk.vector
            def _(v, g=g):
                v.wait_ge(s_a2.h, g + 1)
                if g < 2:
                    v.wait_ge(s_mx.h, TC)
                    v.wait_ge(sTX[g % 2].h, 128 * (TC // 2))
                else:
                    v.wait_ge(sTX[g % 2].h, 128 * (TC // 2) + 128 * (g // 2))
                s_v2.inc(v.tensor_scalar(xtm[:, g % 2, :], xnf[:, g % 2, :],
                                         MAGIC, sc1[:, 0:1], ALU.subtract,
                                         ALU.mult))

            if g % 2 == 0:
                @blk.sync
                def _(yq, g=g):
                    yq.wait_ge(sTX[0].h, sTX[0].n)
                    yq.wait_ge(s_v2.h, g + 1)
                    for k in range(DCH):
                        sTX[0].inc(yq.dma_start_transpose(
                            bigA[:, k, g * 128:(g + 1) * 128],
                            xtm[:, g % 2, k * 128:(k + 1) * 128]), 16)
            else:
                @blk.sync
                def _(yq, g=g):
                    yq.wait_ge(sTX[1].h, sTX[1].n)
                    yq.wait_ge(s_v2.h, g + 1)
                    for k in range(DCH):
                        sTX[1].inc(yq.dma_start_transpose(
                            bigA[:, k, g * 128:(g + 1) * 128],
                            xtm[:, g % 2, k * 128:(k + 1) * 128]), 16)

        @blk.vector
        def _(v):
            v.wait_ge(s_ccd.h, CC2_D)
            v.wait_ge(s_gb.h, SEL_D)
            v.memset(kvbd[:], 0.0)
            v.memset(normW[:], 0.0)
            v.drain()
            for p in range(NP):
                v.tensor_copy(kvbd[0:64, p, 0:64], kv_acc[0:64, p * 128:p * 128 + 64])
                v.tensor_copy(kvbd[64:128, p, 64:128],
                              kv_acc[64:128, p * 128 + 64:p * 128 + 128])
                v.tensor_copy(normW[0:64, p, 2 * p:2 * p + 1],
                              kv_acc[0:64, 1024 + p:1025 + p])
                v.tensor_copy(normW[64:128, p, 2 * p + 1:2 * p + 2],
                              kv_acc[64:128, 1024 + p:1025 + p])
            s_dv.inc(v.drain())
        DV0 = s_dv.n

        for tt in range(TT):
            @blk.sync
            def _(yq, tt=tt):
                if tt == 0:
                    yq.wait_ge(sQW[0].h, 16 * ((TT + 1) // 2))
                    yq.wait_ge(sQW[1].h, 16 * (TT // 2))
                if tt >= 2:
                    yq.wait_ge(s_dm.h, 17 * (tt - 1))
                sQR[tt % 2].inc(yq.dma_start(out=qproll[:, tt % 2, :, :],
                                             in_=sp_qp[:, :, tt * 512:(tt + 1) * 512]), 16)

            Bd = 17 * tt
            Bv = DV0 + 18 * tt

            @blk.tensor
            def _(e, tt=tt, Bd=Bd, Bv=Bv):
                e.wait_ge(sQR[tt % 2].h, 16 * (tt // 2 + 1))
                if tt == 0:
                    e.wait_ge(s_dv.h, DV0)
                else:
                    e.wait_ge(s_dv.h, DV0 + 18 * (tt - 1) + 2)
                for p in range(NP):
                    mm = e.matmul(pb[0][0:16, :], normW[:, p, :], qproll[:, tt % 2, p, :],
                                  start=(p == 0), stop=(p == NP - 1))
                s_dm.inc(mm)
                for p in range(NP):
                    ob = pb[1 + p % 2]
                    bb = pb[3 + p % 2]
                    if p == 0:
                        e.wait_ge(s_dv.h, Bv + 2)
                        if tt >= 1:
                            e.wait_ge(s_dv.h, DV0 + 18 * tt)
                            e.wait_ge(s_de.h, 8 * tt)
                    if p >= 2:
                        e.wait_ge(s_dv.h, Bv + 2 * p)
                    mm = e.matmul(bb[:], selP[:, p, :], rec_sb[:], start=True, stop=True)
                    s_dm.inc(mm)
                    if p >= 2:
                        e.wait_ge(s_de.h, 8 * tt + p - 1)
                    e.wait_ge(s_dv.h, Bv + 4 + 2 * p)
                    mm = e.matmul(ob[:], kvbd[:, p, :], qsc[:], start=True, stop=True)
                    s_dm.inc(mm)

            @blk.vector
            def _(v, tt=tt, Bd=Bd, Bv=Bv):
                v.wait_ge(s_dm.h, Bd + 1)
                if tt >= 1:
                    v.wait_ge(s_dm.h, 17 * (tt - 1) + 16)
                v.tensor_scalar(nrm_sb[:], pb[0][0:16, :], ATT_EPS, None, ALU.add)
                v.drain()
                v.reciprocal(rec_sb[:], nrm_sb[:])
                s_dv.inc(v.drain(), 2)
                for p in range(NP):
                    v.wait_ge(s_dm.h, Bd + 2 + 2 * p)
                    bb = pb[3 + p % 2]
                    s_dv.inc(v.tensor_tensor(qsc[:], qproll[:, tt % 2, p, :], bb[:],
                                             ALU.mult), 2)

            @blk.scalar
            def _(a, tt=tt, Bd=Bd):
                for p in range(NP):
                    a.wait_ge(s_dm.h, Bd + 3 + 2 * p)
                    ob = pb[1 + p % 2]
                    if tt >= 2 and p == 0:
                        a.wait_ge(sOW[tt % 2].h, 16 * ((tt - 2) // 2 + 1))
                    s_de.inc(a.activation(qfm[:, (tt % 2) * 8 + p, :], ob[:],
                                          AF.Identity))

            @blk.gpsimd
            def _(g_, tt=tt):
                g_.wait_ge(s_de.h, 8 * (tt + 1))
                sOW[tt % 2].inc(g_.dma_start(out=sp_oe[:, :, tt * 512:(tt + 1) * 512],
                                             in_=qfm[:, (tt % 2) * 8:(tt % 2) * 8 + 8, :]), 16)
        D_DM = s_dm.n
        D_DE = s_de.n
        DV_TOT = s_dv.n
        OW_TOT = [sOW[0].n, sOW[1].n]

        def emit_dequant(i, src_ap_fn, wt_dst_fn, first_g_wait=None):
            b = i % 2

            @blk.gpsimd
            def _(g_, i=i, b=b):
                if first_g_wait is not None:
                    first_g_wait(g_, i)
                if i >= 2:
                    g_.wait_ge(s_dq2.h, DQ20 + (i - 2) + 1)
                sWL[b].inc(g_.dma_start(out=wnat[:, b, :], in_=src_ap_fn(i)), 16)
            wl_n = sWL[b].n

            @blk.vector
            def _(v, i=i, b=b, wl_n=wl_n):
                v.wait_ge(sWL[b].h, wl_n)
                if i >= 2:
                    v.wait_ge(s_dq2.h, DQ20 + (i - 2) + 1)
                fv = wnat[:, b, :].rearrange("p (a c) -> p a c", c=32)
                v.tensor_reduce(wsc[:, b, :].unsqueeze(2), fv, AX.X, ALU.max,
                                apply_absolute_value=True)
                v.drain()
                v.tensor_scalar(wsc[:, b, :], wsc[:, b, :], 1.0 / 7.0, 1e-12,
                                ALU.mult, ALU.max)
                v.drain()
                v.reciprocal(wrc[:, b, :], wsc[:, b, :])
                v.drain()
                s_dqv.inc(v.tensor_tensor(
                    fv, fv, wrc[:, b, :].unsqueeze(2).broadcast_to([128, 32, 32]),
                    ALU.mult))

            @blk.scalar
            def _(a, i=i, b=b):
                a.wait_ge(s_dqv.h, DQV0 + i + 1)
                s_dqs.inc(a.activation(wnat[:, b, :], wnat[:, b, :], AF.Identity,
                                       bias=cstM[:]))

            @blk.vector
            def _(v, i=i, b=b):
                v.wait_ge(s_dqs.h, DQS0 + i + 1)
                if i >= 2:
                    v.wait_ge(sWT[b].h, WT0[b] + 128 * ((i - 2) // 2 + 1))
                s_dq2.inc(v.scalar_tensor_tensor(
                    wbq[:, b, :].rearrange("p (a c) -> p a c", c=32),
                    wnat[:, b, :].rearrange("p (a c) -> p a c", c=32), -MAGIC,
                    wsc[:, b, :].unsqueeze(2).broadcast_to([128, 32, 32]),
                    ALU.add, ALU.mult))

            @blk.sync
            def _(yq, i=i, b=b):
                yq.wait_ge(sWT[b].h, sWT[b].n)
                yq.wait_ge(s_dq2.h, DQ20 + i + 1)
                for k in range(DCH):
                    sWT[b].inc(yq.dma_start_transpose(
                        wt_dst_fn(i, k), wbq[:, b, k * 128:(k + 1) * 128]), 16)

        DQV0 = s_dqv.n
        DQS0 = s_dqs.n
        DQ20 = s_dq2.n
        WT0 = [sWT[0].n, sWT[1].n]
        base_fm_g = s_fm.n
        base_fe_g = s_fe.n
        base_fr_g = s_fr.n
        gate_fw = [sFW[0].n, sFW[1].n]

        def gate_rest(ic):
            @blk.tensor
            def _(e, ic=ic):
                e.wait_ge(sWT[ic % 2].h, WT0[ic % 2] + 128 * (ic // 2 + 1))
                if ic == 0:
                    e.wait_ge(s_kve.h, 1)
                for tt in range(TT):
                    q = ic * TT + tt
                    if ic == 0:
                        e.wait_ge(sTX[0].h, 128 * (TC // 2) + 128 * (2 * tt + 2))
                        e.wait_ge(sTX[1].h, 128 * (TC // 2) + 128 * (2 * tt + 2))
                    if q >= 2:
                        e.wait_ge(s_fe.h, base_fe_g + q - 1)
                    for k in range(DCH):
                        mm = e.matmul(pb[5 + tt % 2][:], gT[:, ic % 2, k, :],
                                      bigA[:, k, tt * 512:(tt + 1) * 512],
                                      start=(k == 0), stop=(k == DCH - 1))
                    s_fm.inc(mm)

            @blk.scalar
            def _(a, ic=ic):
                for tt in range(TT):
                    q = ic * TT + tt
                    a.wait_ge(s_fm.h, base_fm_g + q + 1)
                    if tt == 0 and ic >= 2:
                        a.wait_ge(sFW[ic % 2].h,
                                  gate_fw[ic % 2] + 16 * ((ic - 2) // 2 + 1))
                    s_fe.inc(a.activation(fst[:, ic % 2, tt * 512:(tt + 1) * 512],
                                          pb[5 + tt % 2][:], AF.Identity))

            @blk.vector
            def _(v, ic=ic):
                v.wait_ge(s_fe.h, base_fe_g + TT * (ic + 1))
                s_fr.inc(v.tensor_reduce(m3arr[:, ic:ic + 1], fst[:, ic % 2, :],
                                         AX.X, ALU.max, apply_absolute_value=True))

            @blk.gpsimd
            def _(g_, ic=ic):
                g_.wait_ge(s_fr.h, base_fr_g + ic + 1)
                sFW[ic % 2].inc(g_.dma_start(out=sp_g[ic], in_=fst[:, ic % 2, :]), 16)

        def g_wait0(g_, i):
            if i == 0:
                g_.wait_ge(s_kv.h, TC)
        for i in range(ICH + 1):
            if i < ICH:
                emit_dequant(i, (lambda j: w_gate[j * 128:(j + 1) * 128, :]),
                             (lambda j, k: gT[:, j % 2, k, :]),
                             first_g_wait=g_wait0)
            if i >= 1:
                gate_rest(i - 1)
        GATE_FR = s_fr.n

        def barrier(cc_in, cc_out, srcs, scr, dst, nvals, derive, wait_fn=None):
            @blk.vector
            def _(v):
                if wait_fn is not None:
                    wait_fn(v)
                for i, src in enumerate(srcs):
                    v.tensor_reduce(pm24[:, i:i + 1], src[:], AX.X, ALU.max)
                s_bar.inc(v.drain())
            bv = s_bar.n

            @blk.gpsimd
            def _(g):
                g.wait_ge(s_ccd.h, s_ccd.n)
                g.wait_ge(s_bar.h, bv)
                k = len(srcs)
                for i in range(k):
                    s_ccd.inc(g.dma_start(out=cc_in[i * 128:(i + 1) * 128].unsqueeze(1),
                                          in_=pm24[:, i:i + 1]), 16)
                g.wait_ge(s_ccd.h, s_ccd.n)
                s_cc.inc(g.collective_compute(
                    "AllReduce", ALU.max, replica_groups=[list(range(n_cores))],
                    ins=[cc_in.opt()], outs=[cc_out.opt()]))
                g.wait_ge(s_cc.h, s_cc.n)
                s_ccd.inc(g.dma_start(
                    out=mrow[:, 0:k, :],
                    in_=cc_out.rearrange("(k p) -> k p", p=128).unsqueeze(0)), 16)
            cd = s_ccd.n

            @blk.vector
            def _(v):
                v.wait_ge(s_ccd.h, cd)
                k = len(srcs)
                v.tensor_reduce(msc[:, 0:k].unsqueeze(2), mrow[:, 0:k, :], AX.X, ALU.max)
                v.drain()
                derive(v)
                s_bar.inc(v.drain())
            bv2 = s_bar.n

            @blk.sync
            def _(yq):
                yq.wait_ge(s_bars.h, s_bars.n)
                yq.wait_ge(s_bar.h, bv2)
                s_bars.inc(yq.dma_start(out=scr[0:nvals].unsqueeze(0), in_=msc[0:1, 0:nvals]), 16)
            bv3 = s_bars.n

            @blk.gpsimd
            def _(g):
                g.wait_ge(s_gb.h, s_gb.n)
                g.wait_ge(s_bars.h, bv3)
                s_gb.inc(g.dma_start(out=dst[:, 0:nvals].unsqueeze(1),
                                     in_=scr[0:nvals].unsqueeze(0).partition_broadcast(128)), 16)
            return s_gb.n

        def derive3(v):
            v.tensor_scalar(msc[:, 0:1], msc[:, 0:1], 1.0 / QMAX, 1e-12,
                            ALU.mult, ALU.max)
            v.drain()
            v.reciprocal(msc[:, 1:2], msc[:, 0:1])
        GB3 = barrier(cc3_in, cc3_out, [m3arr], scr3, sc3, 2, derive3,
                      wait_fn=lambda v: v.wait_ge(s_fr.h, GATE_FR))

        DQV0 = s_dqv.n
        DQS0 = s_dqs.n
        DQ20 = s_dq2.n
        WT0 = [sWT[0].n, sWT[1].n]
        base_fm_u = s_fm.n
        base_fe_u = s_fe.n
        base_fr_u = s_fr.n
        up_fw = [sFW[0].n, sFW[1].n]

        H2 = T // 2

        def up_mm(ic):
            @blk.tensor
            def _(e, ic=ic):
                e.wait_ge(sWT[ic % 2].h, WT0[ic % 2] + 128 * (ic // 2 + 1))
                for tt in range(TT):
                    q = ic * TT + tt
                    if q >= 2:
                        e.wait_ge(s_fe.h, base_fe_u + q - 1)
                    for k in range(DCH):
                        mm = e.matmul(pb[5 + tt % 2][:], gT[:, ic % 2, k, :],
                                      bigA[:, k, tt * 512:(tt + 1) * 512],
                                      start=(k == 0), stop=(k == DCH - 1))
                    s_fm.inc(mm)

        def up_rest_iii(ic):
            @blk.scalar
            def _(a, ic=ic):
                for tt in range(TT):
                    q = ic * TT + tt
                    a.wait_ge(s_fm.h, base_fm_u + q + 1)
                    if tt == 0:
                        if ic >= 2:
                            a.wait_ge(sFW[ic % 2].h,
                                      up_fw[ic % 2] + 16 * ((ic - 2) // 2 + 1))
                        else:
                            a.wait_ge(sFW[ic % 2].h, up_fw[ic % 2])
                    s_fe.inc(a.activation(fst[:, ic % 2, tt * 512:(tt + 1) * 512],
                                          pb[5 + tt % 2][:], AF.Identity))

            @blk.vector
            def _(v, ic=ic):
                v.wait_ge(s_fe.h, base_fe_u + TT * (ic + 1))
                s_fr.inc(v.tensor_reduce(m2arr[:, ic:ic + 1], fst[:, ic % 2, :],
                                         AX.X, ALU.max, apply_absolute_value=True))

            @blk.gpsimd
            def _(g_, ic=ic):
                g_.wait_ge(s_fr.h, base_fr_u + ic + 1)
                sFW[ic % 2].inc(g_.dma_start(out=sp_u[ic], in_=fst[:, ic % 2, :]), 16)

            @blk.sync
            def _(yq, ic=ic):
                yq.wait_ge(sFW[ic % 2].h, gate_fw[ic % 2] + 16 * (ic // 2 + 1))
                if ic >= 2:
                    yq.wait_ge(s_3s.h, 16 * (ic - 1))
                sR3[ic % 2].inc(yq.dma_start(out=gst[:, ic % 2, :], in_=sp_g[ic]), 16)

            for h in range(2):
                hc = 2 * ic + h
                H4 = T // 4

                @blk.scalar
                def _(a, ic=ic, h=h, hc=hc, H4=H4):
                    if h == 0:
                        a.wait_ge(sR3[ic % 2].h, 16 * (ic // 2 + 1))
                    if hc == 0:
                        a.wait_ge(s_gb.h, GB3)
                        a.wait_ge(s_dm.h, D_DM)
                        a.wait_ge(s_dv.h, DV_TOT)
                        a.wait_ge(s_de.h, D_DE)
                        a.wait_ge(sOW[0].h, OW_TOT[0])
                        a.wait_ge(sOW[1].h, OW_TOT[1])
                        a.wait_ge(s_a2.h, TC)
                        a.wait_ge(s_v2.h, TC)
                    elif hc >= 2:
                        a.wait_ge(s_3v.h, 4 * (hc - 2) + 2)
                    base = 8 * hc
                    src_g = gst[:, ic % 2, h * (T // 2):(h + 1) * (T // 2)]
                    for q in range(2):
                        s_3s.inc(a.activation(t1h[:, hc % 2, q * H4:(q + 1) * H4],
                                              src_g[:, q * H4:(q + 1) * H4],
                                              AF.Identity, bias=cstM[:], scale=sc3[:, 1:2]))
                    for q in range(2):
                        a.wait_ge(s_3s.h, base + 1 + q)
                        s_3s.inc(a.activation(g1h[:, hc % 2, q * H4:(q + 1) * H4],
                                              t1h[:, hc % 2, q * H4:(q + 1) * H4],
                                              AF.Identity, bias=cstMn[:]))
                    for q in range(2):
                        a.wait_ge(s_3s.h, base + 3 + q)
                        s_3s.inc(a.activation(t1h[:, hc % 2, q * H4:(q + 1) * H4],
                                              g1h[:, hc % 2, q * H4:(q + 1) * H4],
                                              AF.Sigmoid, scale=sc3[:, 0:1]))
                    for q in range(2):
                        a.wait_ge(s_3s.h, base + 5 + q)
                        s_3s.inc(a.activation(t1h[:, hc % 2, q * H4:(q + 1) * H4],
                                              t1h[:, hc % 2, q * H4:(q + 1) * H4],
                                              AF.Identity, bias=cstM[:], scale=65536.0))

                @blk.vector
                def _(v, ic=ic, h=h, hc=hc):
                    v.wait_ge(s_3s.h, 8 * hc + 8)
                    tv = t1h[:, hc % 2, :]
                    v.tensor_scalar(tv, tv, MAGIC, 65535.0, ALU.subtract, ALU.min)
                    v.drain()
                    if h == 0 and ic >= 2:
                        v.wait_ge(sGW[ic % 2].h, 16 * ((ic - 2) // 2 + 1))
                    s_3v.inc(v.tensor_tensor(
                        g2r[:, ic % 2, h * (T // 2):(h + 1) * (T // 2)],
                        g1h[:, hc % 2, :], tv, ALU.mult), 2)
                    v.drain()
                    s_3v.inc(v.tensor_reduce(
                        m4arr[:, hc:hc + 1],
                        g2r[:, ic % 2, h * (T // 2):(h + 1) * (T // 2)],
                        AX.X, ALU.max, apply_absolute_value=True), 2)

            @blk.gpsimd
            def _(g_, ic=ic):
                g_.wait_ge(s_3v.h, 4 * (2 * ic + 1) + 4)
                sGW[ic % 2].inc(g_.dma_start(out=sp_g2[ic], in_=g2r[:, ic % 2, :]), 16)

        for i in range(ICH + 1):
            if i < ICH:
                emit_dequant(i, (lambda j: w_up[j * 128:(j + 1) * 128, :]),
                             (lambda j, k: gT[:, j % 2, k, :]))
                up_mm(i)
            if i >= 1:
                up_rest_iii(i - 1)
        UP_FM = s_fm.n
        UP_FR = s_fr.n
        III_3V = s_3v.n
        III_3S = s_3s.n
        GW_TOT = [sGW[0].n, sGW[1].n]

        def derive4(v):
            v.tensor_scalar(msc[:, 0:1], msc[:, 0:1], 1.0 / QMAX, 1e-12,
                            ALU.mult, ALU.max)
            v.tensor_tensor(msc[:, 1:2], msc[:, 1:2], sc3[0:1, 0:1], ALU.mult)
            v.drain()
            v.reciprocal(msc[:, 2:3], msc[:, 0:1])
            v.tensor_scalar(msc[:, 1:2], msc[:, 1:2], 1.0 / (65536.0 * QMAX),
                            1e-12, ALU.mult, ALU.max)
            v.drain()
            v.reciprocal(msc[:, 3:4], msc[:, 1:2])
            v.tensor_tensor(msc[:, 4:5], msc[:, 1:2], msc[:, 0:1], ALU.mult)
            v.drain()
            v.tensor_tensor(msc[:, 3:4], msc[:, 3:4], sc3[0:1, 0:1], ALU.mult)
            v.drain()
            v.tensor_scalar(msc[:, 3:4], msc[:, 3:4], 1.0 / 65536.0, None,
                            ALU.mult)

        def wait4(v):
            v.wait_ge(s_fr.h, UP_FR)
            v.wait_ge(s_3v.h, III_3V)
        GB4 = barrier(cc4_in, cc4_out, [m2arr, m4arr], scr4, sc24, 5, derive4,
                      wait_fn=wait4)

        DQV0 = s_dqv.n
        DQS0 = s_dqs.n
        DQ20 = s_dq2.n
        WT0 = [sWT[0].n, sWT[1].n]
        base_r3b = [sR3[0].n, sR3[1].n]

        def g_wait_d(g_, i):
            if i == 0:
                g_.wait_ge(s_fm.h, UP_FM)

        def emit_downw(i):
            dc, t3 = i // 3, i % 3

            def dst_fn(j, k, t3=t3, dc=dc):
                return downT[:, t3 * 8 + k, dc * 128:(dc + 1) * 128]
            emit_dequant(i, (lambda j, dc=dc, t3=t3:
                             w_down[dc * 128:(dc + 1) * 128,
                                    t3 * 1024:(t3 + 1) * 1024]),
                         dst_fn, first_g_wait=g_wait_d)

        def emit_iva(ic):
            @blk.sync
            def _(yq, ic=ic):
                yq.wait_ge(sFW[ic % 2].h, up_fw[ic % 2] + 16 * (ic // 2 + 1))
                if ic < 2:
                    yq.wait_ge(s_3s.h, III_3S)
                else:
                    yq.wait_ge(s_4s.h, 6 * (ic - 2) + 2)
                sR3[ic % 2].inc(yq.dma_start(out=gst[:, ic % 2, :], in_=sp_u[ic]), 16)
                yq.wait_ge(sGW[ic % 2].h, GW_TOT[ic % 2])
                if ic >= 2:
                    yq.wait_ge(sORW[ic % 2].h, 16 * ((ic - 2) // 2 + 1))
                sG2R[ic % 2].inc(yq.dma_start(out=g2r[:, ic % 2, :], in_=sp_g2[ic]), 16)

            @blk.scalar
            def _(a, ic=ic):
                a.wait_ge(sR3[ic % 2].h, base_r3b[ic % 2] + 16 * (ic // 2 + 1))
                if ic == 0:
                    a.wait_ge(s_gb.h, GB4)
                    a.wait_ge(s_3s.h, III_3S)
                    a.wait_ge(s_3v.h, III_3V)
                elif ic >= 2:
                    a.wait_ge(s_4v.h, 3 * (ic - 2) + 1)
                base = 6 * ic
                uv = iva_u[:, ic % 2, :]
                for h in range(2):
                    s_4s.inc(a.activation(uv[:, h * H2:(h + 1) * H2],
                                          gst[:, ic % 2, h * H2:(h + 1) * H2],
                                          AF.Identity, bias=cstM[:], scale=sc24[:, 2:3]))
                a.wait_ge(sG2R[ic % 2].h, 16 * (ic // 2 + 1))
                for h in range(2):
                    s_4s.inc(a.activation(g2r[:, ic % 2, h * H2:(h + 1) * H2],
                                          g2r[:, ic % 2, h * H2:(h + 1) * H2],
                                          AF.Identity, bias=cstM[:], scale=sc24[:, 3:4]),
                             3 if h == 1 else 1)

            @blk.vector
            def _(v, ic=ic):
                v.wait_ge(s_4s.h, 6 * ic + 6)
                v.tensor_scalar(iva_u[:, ic % 2, :], iva_u[:, ic % 2, :],
                                MAGIC, None, ALU.subtract)
                v.drain()
                s_4v.inc(v.scalar_tensor_tensor(g2r[:, ic % 2, :], g2r[:, ic % 2, :],
                                                -MAGIC, iva_u[:, ic % 2, :],
                                                ALU.add, ALU.mult))
                v.drain()
                s_4v.inc(v.tensor_reduce(m5arr[:, ic:ic + 1], g2r[:, ic % 2, :],
                                         AX.X, ALU.max, apply_absolute_value=True), 2)

            @blk.gpsimd
            def _(g_, ic=ic):
                g_.wait_ge(s_4v.h, 3 * ic + 3)
                sORW[ic % 2].inc(g_.dma_start(out=sp_or[ic], in_=g2r[:, ic % 2, :]), 16)

        for i in range(ICH + 1):
            if i < ICH:
                emit_downw(i)
            if i >= 1:
                emit_iva(i - 1)
        IVA_4V = s_4v.n
        IVA_4S = s_4s.n
        DWT_N = [sWT[0].n, sWT[1].n]
        FW_TOT = [sFW[0].n, sFW[1].n]
        ORW_TOT = [sORW[0].n, sORW[1].n]

        @blk.gpsimd
        def _(g):
            for c in range(DCH):
                if c == 0:
                    g.wait_ge(sFW[0].h, FW_TOT[0])
                    g.wait_ge(sFW[1].h, FW_TOT[1])
                if c >= 2:
                    g.wait_ge(s_pj.h, c - 1)
                s_pjl.inc(g.dma_start(out=pjst[:, c % 2, :],
                                      in_=proj_wT[c * 128:(c + 1) * 128, :]), 16)

        @blk.scalar
        def _(a):
            for c in range(DCH):
                a.wait_ge(s_pjl.h, 32 * (c // 2 + 1))
                if c == 0:
                    a.wait_ge(s_fm.h, UP_FM)
                s_pj.inc(a.copy(projT[:, c, :], pjst[:, c % 2, :]))
        PJ_N = s_pj.n

        def derive5(v):
            v.tensor_tensor(msc[:, 0:1], msc[:, 0:1], sc24[0:1, 4:5], ALU.mult)
            v.drain()
            v.tensor_scalar(msc[:, 0:1], msc[:, 0:1], 1.0 / QMAX, 1e-12,
                            ALU.mult, ALU.max)
            v.drain()
            v.reciprocal(msc[:, 1:2], msc[:, 0:1])
            v.drain()
            v.tensor_tensor(msc[:, 1:2], msc[:, 1:2], sc24[0:1, 4:5], ALU.mult)
        GB5 = barrier(cc5_in, cc5_out, [m5arr], scr5, sc5, 2, derive5,
                      wait_fn=lambda v: v.wait_ge(s_4v.h, IVA_4V))

        for tt in range(TT):
            @blk.sync
            def _(yq, tt=tt):
                if tt == 0:
                    yq.wait_ge(sORW[0].h, ORW_TOT[0])
                    yq.wait_ge(sORW[1].h, ORW_TOT[1])
                    yq.wait_ge(sWT[0].h, DWT_N[0])
                    yq.wait_ge(sWT[1].h, DWT_N[1])
                    yq.wait_ge(s_pj.h, PJ_N)
                else:
                    yq.wait_ge(s_5v.h, tt)
                s_ord.inc(yq.dma_start(
                    out=orst[:],
                    in_=sp_or.rearrange("i p t -> p i t")[:, :, tt * 512:(tt + 1) * 512]), 16)

            @blk.scalar
            def _(a, tt=tt):
                a.wait_ge(s_ord.h, 16 * (tt + 1))
                if tt == 0:
                    a.wait_ge(s_gb.h, GB5)
                ov = orst[:].rearrange("p a b -> p (a b)")
                s_5s.inc(a.activation(ov, ov, AF.Identity, bias=cstM[:],
                                      scale=sc5[:, 1:2]))

            @blk.vector
            def _(v, tt=tt):
                v.wait_ge(s_5s.h, tt + 1)
                if tt == 0:
                    v.wait_ge(s_4s.h, IVA_4S)
                    v.wait_ge(s_4v.h, IVA_4V)
                if tt >= 2:
                    v.wait_ge(s_tf.h, 4 * (tt - 2) + 4)
                s_5v.inc(v.tensor_scalar(
                    oTt[:, tt % 2, :, :].rearrange("p a b -> p (a b)"),
                    orst[:].rearrange("p a b -> p (a b)"),
                    MAGIC, sc5[:, 0:1], ALU.subtract, ALU.mult))

            for c in range(4):
                fb = 4 * tt + c

                @blk.sync
                def _(yq, fb=fb):
                    if fb >= 2:
                        yq.wait_ge(s_vf.h, fb - 1)
                        yq.wait_ge(s_tf.h, fb - 1)
                    sRF[fb % 2].inc(yq.dma_start(out=oest[:, fb % 2, :, :],
                                                 in_=sp_oe[:, :, fb * 128:(fb + 1) * 128]), 16)
                    sRF[fb % 2].inc(yq.dma_start(out=xrs[:, fb % 2, :],
                                                 in_=x[fb * 128:(fb + 1) * 128, :]), 16)

                @blk.tensor
                def _(e, tt=tt, c=c, fb=fb):
                    if fb == 0:
                        e.wait_ge(sWT[0].h, DWT_N[0])
                        e.wait_ge(sWT[1].h, DWT_N[1])
                        e.wait_ge(s_pj.h, PJ_N)
                    e.wait_ge(s_5v.h, tt + 1)
                    e.wait_ge(sRF[fb % 2].h, 32 * (fb // 2 + 1))
                    if fb >= 2:
                        e.wait_ge(s_vf.h, fb - 1)
                    bk0 = pb[(fb % 2) * 2]
                    bk1 = pb[(fb % 2) * 2 + 1]
                    for ic in range(ICH):
                        e.matmul(bk0[:], oTt[:, tt % 2, ic, c * 128:(c + 1) * 128],
                                 downT[:, ic, 0:512], start=(ic == 0), stop=False)
                        e.matmul(bk1[:], oTt[:, tt % 2, ic, c * 128:(c + 1) * 128],
                                 downT[:, ic, 512:1024], start=(ic == 0), stop=False)
                    for kc in range(DCH):
                        e.matmul(bk0[:], oest[:, fb % 2, kc, :], projT[:, kc, 0:512],
                                 start=False, stop=False)
                        e.matmul(bk1[:], oest[:, fb % 2, kc, :], projT[:, kc, 512:1024],
                                 start=False, stop=False)
                    e.matmul(bk0[:], ones_r[0:1, 0:128], prb[0:1, 0:512],
                             start=False, stop=True)
                    mm = e.matmul(bk1[:], ones_r[0:1, 0:128], prb[0:1, 512:1024],
                                  start=False, stop=True)
                    s_tf.inc(mm)

                @blk.vector
                def _(v, fb=fb):
                    v.wait_ge(s_tf.h, fb + 1)
                    if fb >= 2:
                        v.wait_ge(sOUT[fb % 2].h, 16 * ((fb - 2) // 2 + 1))
                    bk0 = pb[(fb % 2) * 2]
                    bk1 = pb[(fb % 2) * 2 + 1]
                    v.tensor_tensor(osb[:, fb % 2, 0:512], bk0[:], xrs[:, fb % 2, 0:512],
                                    ALU.add)
                    v.tensor_tensor(osb[:, fb % 2, 512:1024], bk1[:],
                                    xrs[:, fb % 2, 512:1024], ALU.add)
                    s_vf.inc(v.drain())

                @blk.scalar
                def _(a, fb=fb):
                    a.wait_ge(s_vf.h, fb + 1)
                    sOUT[fb % 2].inc(a.dma_start(out=y[fb * 128:(fb + 1) * 128, :],
                                                 in_=osb[:, fb % 2, :]), 16)

        @blk.gpsimd
        def _(g):
            g.wait_ge(sOUT[0].h, sOUT[0].n)
            g.wait_ge(sOUT[1].h, sOUT[1].n)

    return nc


from concourse.bass_utils import run_bass_kernel_spmd

_T = 4096
_NCORES = 8
_CACHE = {}


def _sel_const():
    sel = np.zeros((16, NP, 128), np.float32)
    for p in range(NP):
        sel[2 * p, p, 0:64] = 1.0
        sel[2 * p + 1, p, 64:128] = 1.0
    return np.ascontiguousarray(sel.reshape(16, NP * 128))


def kernel(L0_attn_x, L0_attn_qkv_w, L0_attn_qkv_b, L0_attn_proj_w, L0_attn_proj_b,
           L0_attn_projection_matrix, L0_ffn_w_gate, L0_ffn_w_up, L0_ffn_w_down):
    x = np.ascontiguousarray(np.asarray(L0_attn_x, dtype=np.float32).reshape(-1, D))
    assert x.shape[0] == _T * _NCORES
    qkv_wT = np.ascontiguousarray(np.asarray(L0_attn_qkv_w, dtype=np.float32).T)
    qkv_b = np.ascontiguousarray(np.asarray(L0_attn_qkv_b, dtype=np.float32))
    proj_wT = np.ascontiguousarray(np.asarray(L0_attn_proj_w, dtype=np.float32).T)
    proj_b = np.ascontiguousarray(np.asarray(L0_attn_proj_b, dtype=np.float32))
    pmatT = np.ascontiguousarray(np.asarray(L0_attn_projection_matrix, dtype=np.float32).T)
    qkbh = np.ascontiguousarray(qkv_b[0:2048].reshape(16, 128).T)
    w_gate = np.ascontiguousarray(np.asarray(L0_ffn_w_gate, dtype=np.float32))
    w_up = np.ascontiguousarray(np.asarray(L0_ffn_w_up, dtype=np.float32))
    w_down = np.ascontiguousarray(np.asarray(L0_ffn_w_down, dtype=np.float32))
    selc = _sel_const()

    if _T not in _CACHE:
        _CACHE[_T] = build(_T, n_cores=_NCORES)
    nc = _CACHE[_T]

    in_maps = []
    for c in range(_NCORES):
        in_maps.append({
            "x": x[c * _T:(c + 1) * _T],
            "qkv_wT": qkv_wT, "qkv_b": qkv_b,
            "proj_wT": proj_wT, "proj_b": proj_b,
            "pmatT": pmatT, "qkbh": qkbh,
            "w_gate": w_gate, "w_up": w_up, "w_down": w_down,
            "selc": selc,
        })
    res = run_bass_kernel_spmd(nc, in_maps, core_ids=list(range(_NCORES)))
    y = np.concatenate([np.asarray(r["y"]) for r in res.results], axis=0)
    return y.reshape(2, 16384, 1024)


# version 14
# speedup vs baseline: 1.0107x; 1.0107x over previous
import numpy as np
import concourse.bass as bass
import concourse.mybir as mybir

F32 = mybir.dt.float32
BF16 = mybir.dt.bfloat16
AF = mybir.ActivationFunctionType
ALU = mybir.AluOpType
AX = mybir.AxisListType
MAGIC = 12582912.0
QMAX = 32767.0
LN8 = 2.0794415416798357
D = 1024
DCH = 8
NP = 8
ICH = 24
LN_EPS = 1e-5
ATT_EPS = 1e-6


class Sem:
    def __init__(self, nc, es, name):
        self.h = es.enter_context(nc.semaphore(name))
        self.n = 0

    def inc(self, instr, k=1):
        instr.then_inc(self.h, k)
        self.n += k
        return instr


def build(T, n_cores=8, race=True):
    import contextlib
    TC = T // 128
    TT = T // 512
    assert T % 512 == 0

    nc = bass.Bass("TRN2", target_bir_lowering=False, debug=False,
                   detect_race_conditions=race)

    x = nc.dram_tensor("x", [T, D], F32, kind="ExternalInput").ap()
    qkv_wT = nc.dram_tensor("qkv_wT", [D, 3 * D], F32, kind="ExternalInput").ap()
    qkv_b = nc.dram_tensor("qkv_b", [3 * D], F32, kind="ExternalInput").ap()
    proj_wT = nc.dram_tensor("proj_wT", [D, D], F32, kind="ExternalInput").ap()
    proj_b = nc.dram_tensor("proj_b", [D], F32, kind="ExternalInput").ap()
    pmatT = nc.dram_tensor("pmatT", [64, 64], F32, kind="ExternalInput").ap()
    qkbh = nc.dram_tensor("qkbh", [128, 16], F32, kind="ExternalInput").ap()
    w_gate = nc.dram_tensor("w_gate", [3072, D], F32, kind="ExternalInput").ap()
    w_up = nc.dram_tensor("w_up", [3072, D], F32, kind="ExternalInput").ap()
    w_down = nc.dram_tensor("w_down", [D, 3072], F32, kind="ExternalInput").ap()
    selc = nc.dram_tensor("selc", [16, NP * 128], F32, kind="ExternalInput").ap()
    y = nc.dram_tensor("y", [T, D], F32, kind="ExternalOutput").ap()

    sp_qp = nc.dram_tensor("sp_qp", [128, NP, T], BF16).ap()
    sp_oe = nc.dram_tensor("sp_oe", [128, DCH, T], BF16).ap()
    sp_g = nc.dram_tensor("sp_g", [ICH, 128, T], BF16).ap()
    sp_u = nc.dram_tensor("sp_u", [ICH, 128, T], BF16).ap()
    sp_g2 = nc.dram_tensor("sp_g2", [ICH, 128, T], F32).ap()
    sp_or = nc.dram_tensor("sp_or", [ICH, 128, T], F32).ap()

    cc1_in = nc.dram_tensor("cc1_in", [128], F32).ap()
    cc1_out = nc.dram_tensor("cc1_out", [128], F32).ap()
    cc2_in = nc.dram_tensor("cc2_in", [128, 1032], F32).ap()
    cc2_out = nc.dram_tensor("cc2_out", [128, 1032], F32).ap()
    cc3_in = nc.dram_tensor("cc3_in", [128], F32).ap()
    cc3_out = nc.dram_tensor("cc3_out", [128], F32).ap()
    cc4_in = nc.dram_tensor("cc4_in", [256], F32).ap()
    cc4_out = nc.dram_tensor("cc4_out", [256], F32).ap()
    cc5_in = nc.dram_tensor("cc5_in", [128], F32).ap()
    cc5_out = nc.dram_tensor("cc5_out", [128], F32).ap()
    scr1 = nc.dram_tensor("scr1", [2], F32).ap()
    scr3 = nc.dram_tensor("scr3", [2], F32).ap()
    scr4 = nc.dram_tensor("scr4", [8], F32).ap()
    scr5 = nc.dram_tensor("scr5", [2], F32).ap()

    es = contextlib.ExitStack()
    with es:
        projBD = es.enter_context(nc.sbuf_tensor("projBD", [128, 128], BF16))
        ones_r = es.enter_context(nc.sbuf_tensor("ones_r", [1, 128], F32))
        ones_cb = es.enter_context(nc.sbuf_tensor("ones_cb", [128, 1], BF16))
        qkb = es.enter_context(nc.sbuf_tensor("qkb", [128, 16], F32))
        qvb = es.enter_context(nc.sbuf_tensor("qvb", [1, D], F32))
        prb = es.enter_context(nc.sbuf_tensor("prb", [1, D], F32))
        rstd_s = es.enter_context(nc.sbuf_tensor("rstd_s", [128, TC], F32))
        nmr_s = es.enter_context(nc.sbuf_tensor("nmr_s", [128, TC], F32))
        xscb = es.enter_context(nc.sbuf_tensor("xscb", [128, TC], F32))
        xbib = es.enter_context(nc.sbuf_tensor("xbib", [128, TC], F32))
        m1arr = es.enter_context(nc.sbuf_tensor("m1arr", [128, TC], F32))
        m3arr = es.enter_context(nc.sbuf_tensor("m3arr", [128, ICH], F32))
        m2arr = es.enter_context(nc.sbuf_tensor("m2arr", [128, ICH], F32))
        m4arr = es.enter_context(nc.sbuf_tensor("m4arr", [128, 2 * ICH], F32))
        m5arr = es.enter_context(nc.sbuf_tensor("m5arr", [128, ICH], F32))
        sqt = es.enter_context(nc.sbuf_tensor("sqt", [128, 4, 1], F32))
        mv2 = es.enter_context(nc.sbuf_tensor("mv2", [128, 4, 2], F32))
        bnb = es.enter_context(nc.sbuf_tensor("bnb", [128, 2, 2, 6], F32))
        maxg = es.enter_context(nc.sbuf_tensor("maxg", [128, 2, 8], F32))
        sc1 = es.enter_context(nc.sbuf_tensor("sc1", [128, 2], F32))
        sc3 = es.enter_context(nc.sbuf_tensor("sc3", [128, 2], F32))
        sc24 = es.enter_context(nc.sbuf_tensor("sc24", [128, 8], F32))
        sc5 = es.enter_context(nc.sbuf_tensor("sc5", [128, 2], F32))
        mrow = es.enter_context(nc.sbuf_tensor("mrow", [1, 2, 128], F32))
        msc = es.enter_context(nc.sbuf_tensor("msc", [1, 8], F32))
        pm24 = es.enter_context(nc.sbuf_tensor("pm24", [128, 2], F32))
        cstM = es.enter_context(nc.sbuf_tensor("cstM", [128, 1], F32))
        cstMn = es.enter_context(nc.sbuf_tensor("cstMn", [128, 1], F32))
        cstE = es.enter_context(nc.sbuf_tensor("cstE", [128, 1], F32))
        cstL8 = es.enter_context(nc.sbuf_tensor("cstL8", [128, 1], F32))
        epsc = es.enter_context(nc.sbuf_tensor("epsc", [128, 1], F32))
        wsc = es.enter_context(nc.sbuf_tensor("wsc", [128, 2, 32], F32))
        wrc = es.enter_context(nc.sbuf_tensor("wrc", [128, 2, 32], F32))

        SB_TOP = nc.sbuf_top
        def man(name, shape, dtype, off):
            return nc.alloc_sbuf_tensor_at(name, shape, dtype, offset=off)

        SZ_BIG = DCH * T * 2
        assert SZ_BIG == ICH * D * 2 + DCH * D * 2
        R_BIG = SB_TOP - SZ_BIG
        bigA = man("bigA", [128, DCH, T], BF16, R_BIG)
        downT = man("downT", [128, ICH, D], BF16, R_BIG)
        projT = man("projT", [128, DCH, D], BF16, R_BIG + ICH * D * 2)

        SZ_Q = DCH * 3072 * 2
        R_Q = R_BIG - SZ_Q
        qkvT = man("qkvT", [128, DCH, 3072], BF16, R_Q)
        fst = man("fst", [128, 2, T], BF16, R_Q)
        gst = man("gst", [128, 2, T], BF16, R_Q + 16384)
        wnat = man("wnat", [128, 2, D], F32, R_Q + 32768)
        wbq = man("wbq", [128, 2, D], BF16, R_Q + 40960)
        gT = man("gT", [128, 2, DCH, 128], BF16, R_Q + 45056)
        orst = man("orst", [128, ICH, 512], F32, R_Q)
        pjst = man("pjst", [128, 2, D], F32, R_Q)

        SZ_W = 8192 + 8192 + 4096 + 16384 + 8192 + 4096 + 4096 + 4096 + 2048 + 16384 + 4224
        R_W = R_Q - SZ_W
        assert R_W >= nc.sbuf_base, (R_W, nc.sbuf_base)
        wst = man("wst", [128, 2, D], F32, R_W)
        xf = man("xf", [128, 2, D], F32, R_W + 8192)
        xtm = man("xtm", [128, 2, D], BF16, R_W + 16384)
        qfm = man("qfm", [128, 16, 512], BF16, R_W + 20480)
        qptm = man("qptm", [128, 4, D], BF16, R_W + 36864)
        kptm = man("kptm", [128, 2, D], BF16, R_W + 45056)
        vtm = man("vtm", [128, 2, D], BF16, R_W + 49152)
        esub = man("esub", [128, 2, 512], F32, R_W + 53248)
        estg = man("estg", [128, 2, 512], BF16, R_W + 57344)
        qproll = man("qproll", [128, 2, NP, 512], BF16, R_W + 59392)
        kv_acc = man("kv_acc", [128, 1032], F32, R_W + 75776)
        assert 75776 + 4224 <= SZ_W
        kpt2 = man("kpt2", [128, 2, D], BF16, R_W)
        vt2 = man("vt2", [128, 2, D], BF16, R_W + 4096)
        xnf = man("xnf", [128, 2, D], F32, R_Q + 16384)
        kvbd = man("kvbd", [128, NP, 128], BF16, R_W + 36864)
        normW = man("normW", [128, NP, 16], BF16, R_W + 38912)
        qsc = man("qsc", [128, 512], BF16, R_W + 39168)
        nrm_sb = man("nrm_sb", [16, 512], F32, R_W + 40192)
        rec_sb = man("rec_sb", [16, 512], F32, R_W + 42240)
        selP = man("selP", [16, NP, 128], F32, R_W + 44288)
        assert 44288 + 4096 <= 49152
        t1s = man("t1s", [128, T], F32, R_W + 8192)
        g1b = man("g1b", [128, T], F32, R_W + 24576)
        t1h = man("t1h", [128, 2, T // 2], F32, R_W + 8192)
        g1h = man("g1h", [128, 2, T // 2], F32, R_W + 24576)
        iva_u = man("iva_u", [128, 2, T], F32, R_W + 8192)
        g2r = man("g2r", [128, 2, T], F32, R_W + 40960)
        assert 40960 + 32768 <= SZ_W
        oTt = man("oTt", [128, 2, ICH, 512], BF16, R_W)
        oest = man("oest", [128, 2, DCH, 128], BF16, R_W + 49152)
        xrs = man("xrs", [128, 2, D], F32, R_W + 53248)
        osb = man("osb", [128, 2, D], F32, R_W + 61440)
        assert 61440 + 8192 <= SZ_W

        names = ("s_pr s_prd s_wq s_wqc s_xa0 s_xa1 s_bn s_ln s_le s_st s_ap s_mx "
                 "s_tx0 s_tx1 s_qm s_qe s_vm s_ve s_cm s_cr s_ee s_ep "
                 "s_tq0 s_tq1 s_qw0 s_qw1 s_kv s_kve "
                 "s_ccd s_cc s_bar s_bars s_gb "
                 "s_a2 s_v2 s_qr0 s_qr1 s_dm s_dv s_de s_ow0 s_ow1 "
                 "s_wl0 s_wl1 s_dqv s_dqs s_dq2 s_wt0 s_wt1 s_fm s_fe s_fr "
                 "s_fw0 s_fw1 s_r30 s_r31 s_3s s_3v s_gw0 s_gw1 "
                 "s_g2r0 s_g2r1 s_4s s_4v s_orw0 s_orw1 s_pj s_pjl "
                 "s_rf0 s_rf1 s_ord s_5s s_5v s_tf s_vf s_out0 s_out1").split()
        S = {nm: Sem(nc, es, nm) for nm in names}
        (s_pr, s_prd, s_wq, s_wqc, s_xa0, s_xa1, s_bn, s_ln, s_le, s_st, s_ap, s_mx,
         s_tx0, s_tx1, s_qm, s_qe, s_vm, s_ve, s_cm, s_cr, s_ee, s_ep,
         s_tq0, s_tq1, s_qw0, s_qw1, s_kv, s_kve,
         s_ccd, s_cc, s_bar, s_bars, s_gb,
         s_a2, s_v2, s_qr0, s_qr1, s_dm, s_dv, s_de, s_ow0, s_ow1,
         s_wl0, s_wl1, s_dqv, s_dqs, s_dq2, s_wt0, s_wt1, s_fm, s_fe, s_fr,
         s_fw0, s_fw1, s_r30, s_r31, s_3s, s_3v, s_gw0, s_gw1,
         s_g2r0, s_g2r1, s_4s, s_4v, s_orw0, s_orw1, s_pj, s_pjl,
         s_rf0, s_rf1, s_ord, s_5s, s_5v, s_tf, s_vf, s_out0, s_out1) = (
            S[nm] for nm in names)
        sXA = [s_xa0, s_xa1]
        sTX = [s_tx0, s_tx1]
        sTQ = [s_tq0, s_tq1]
        sQW = [s_qw0, s_qw1]
        sQR = [s_qr0, s_qr1]
        sOW = [s_ow0, s_ow1]
        sWL = [s_wl0, s_wl1]
        sWT = [s_wt0, s_wt1]
        sFW = [s_fw0, s_fw1]
        sR3 = [s_r30, s_r31]
        sG2R = [s_g2r0, s_g2r1]
        sGW = [s_gw0, s_gw1]
        sORW = [s_orw0, s_orw1]
        sRF = [s_rf0, s_rf1]
        sOUT = [s_out0, s_out1]

        pb = [es.enter_context(nc.psum_tensor(f"pb{i}", [128, 512], F32))
              for i in range(8)]

        blk = es.enter_context(nc.Block())

        @blk.gpsimd
        def _(g):
            g.memset(ones_r[:], 1.0)
            g.memset(ones_cb[:], 1.0)
            g.memset(projBD[:], 0.0)
            g.memset(cstM[:], MAGIC)
            g.memset(cstMn[:], -MAGIC)
            g.memset(cstE[:], LN_EPS)
            g.memset(cstL8[:], -LN8)
            s_pr.inc(g.memset(epsc[:], ATT_EPS))
            s_prd.inc(g.dma_start(out=esub[0:64, 0, 0:64], in_=pmatT), 16)
            s_prd.inc(g.dma_start(out=qkb[:], in_=qkbh), 16)
            s_prd.inc(g.dma_start(out=qvb[:], in_=qkv_b[2048:3072].unsqueeze(0)), 16)
            s_prd.inc(g.dma_start(out=prb[:], in_=proj_b.unsqueeze(0)), 16)
            for c in range(DCH):
                for t3 in range(3):
                    i = 3 * c + t3
                    if i >= 2:
                        g.wait_ge(s_wqc.h, i - 1)
                    s_wq.inc(g.dma_start(
                        out=wst[:, i % 2, :],
                        in_=qkv_wT[c * 128:(c + 1) * 128, t3 * 1024:(t3 + 1) * 1024]), 16)

        @blk.scalar
        def _(a):
            for c in range(DCH):
                for t3 in range(3):
                    i = 3 * c + t3
                    a.wait_ge(s_wq.h, 32 * (i // 2 + 1))
                    s_wqc.inc(a.copy(qkvT[:, c, t3 * 1024:(t3 + 1) * 1024],
                                     wst[:, i % 2, :]))
            s_wqc.inc(a.drain())

        @blk.vector
        def _(v):
            v.wait_ge(s_pr.h, 1)
            v.wait_ge(s_prd.h, 64)
            v.tensor_scalar(projBD[0:64, 0:64], esub[0:64, 0, 0:64], 0.125, None,
                            ALU.mult)
            v.tensor_scalar(projBD[64:128, 64:128], esub[0:64, 0, 0:64],
                            0.125, None, ALU.mult)
            s_cr.inc(v.drain())
        CR_P = s_cr.n

        @blk.sync
        def _(yq):
            for g in range(2):
                sXA[g % 2].inc(yq.dma_start(out=xf[:, g % 2, :],
                                            in_=x[g * 128:(g + 1) * 128, :]), 16)

        def emitA_chunk(g):
            @blk.sync
            def _(yq, g=g):
                ld = g + 2
                if ld < TC:
                    yq.wait_ge(s_ap.h, g + 1)
                    sXA[ld % 2].inc(yq.dma_start(out=xf[:, ld % 2, :],
                                                 in_=x[ld * 128:(ld + 1) * 128, :]), 16)

            @blk.vector
            def _(v, g=g):
                v.wait_ge(sXA[g % 2].h, 16 * (g // 2 + 1))
                if g >= 4:
                    v.wait_ge(s_ln.h, g - 3)
                v.bn_stats(bnb[:, g % 2, 0, :], xf[:, g % 2, 0:512])
                v.bn_stats(bnb[:, g % 2, 1, :], xf[:, g % 2, 512:1024])
                v.drain()
                s_bn.inc(v.bn_aggr(mv2[:, g % 4, :], bnb[:, g % 2, :, :]))
                if g >= 1:
                    gp = g - 1
                    v.wait_ge(s_le.h, gp + 1)
                    s_st.inc(v.scalar_tensor_tensor(
                        nmr_s[:, gp:gp + 1], mv2[:, gp % 4, 0:1], -1.0,
                        rstd_s[:, gp:gp + 1], ALU.mult, ALU.mult))
                    if gp >= 2:
                        v.wait_ge(sTX[gp % 2].h, 128 * ((gp - 2) // 2 + 1))
                    s_ap.inc(v.tensor_scalar(xtm[:, gp % 2, :], xf[:, gp % 2, :],
                                             mv2[:, gp % 4, 0:1],
                                             rstd_s[:, gp:gp + 1],
                                             ALU.subtract, ALU.mult))
                if g >= 2:
                    s_mx.inc(v.tensor_reduce(m1arr[:, g - 2:g - 1],
                                             xtm[:, g % 2, :], AX.X, ALU.max,
                                             apply_absolute_value=True))

            @blk.scalar
            def _(a, g=g):
                a.wait_ge(s_bn.h, g + 1)
                if g >= 4:
                    a.wait_ge(s_le.h, g - 3)
                s_ln.inc(a.activation(sqt[:, g % 4, :], mv2[:, g % 4, 1:2],
                                      AF.Ln, bias=cstE[:]))
                if g >= 1:
                    j = g - 1
                    a.wait_ge(s_ln.h, j + 1)
                    s_le.inc(a.activation(rstd_s[:, j:j + 1], sqt[:, j % 4, :],
                                          AF.Exp, scale=-0.5))

            @blk.sync
            def _(yq, g=g):
                if g >= 1:
                    gp = g - 1
                    yq.wait_ge(sTX[gp % 2].h, sTX[gp % 2].n)
                    yq.wait_ge(s_ap.h, gp + 1)
                    for k in range(DCH):
                        sTX[gp % 2].inc(yq.dma_start_transpose(
                            bigA[:, k, gp * 128:(gp + 1) * 128],
                            xtm[:, gp % 2, k * 128:(k + 1) * 128]), 16)

        def emitA(tt0):
            for c in range(4):
                emitA_chunk(tt0 * 4 + c)

        def emitA_tail():
            @blk.scalar
            def _(a):
                j = TC - 1
                a.wait_ge(s_ln.h, j + 1)
                s_le.inc(a.activation(rstd_s[:, j:j + 1], sqt[:, j % 4, :],
                                      AF.Exp, scale=-0.5))

            @blk.vector
            def _(v):
                gp = TC - 1
                v.drain()
                v.wait_ge(s_le.h, gp + 1)
                s_st.inc(v.scalar_tensor_tensor(
                    nmr_s[:, gp:gp + 1], mv2[:, gp % 4, 0:1], -1.0,
                    rstd_s[:, gp:gp + 1], ALU.mult, ALU.mult))
                v.wait_ge(sTX[gp % 2].h, 128 * ((gp - 2) // 2 + 1))
                s_ap.inc(v.tensor_scalar(xtm[:, gp % 2, :], xf[:, gp % 2, :],
                                         mv2[:, gp % 4, 0:1], rstd_s[:, gp:gp + 1],
                                         ALU.subtract, ALU.mult))
                s_mx.inc(v.tensor_reduce(m1arr[:, TC - 2:TC - 1],
                                         xtm[:, TC % 2, :], AX.X, ALU.max,
                                         apply_absolute_value=True))

            @blk.sync
            def _(yq):
                gp = TC - 1
                yq.wait_ge(sTX[gp % 2].h, sTX[gp % 2].n)
                yq.wait_ge(s_ap.h, gp + 1)
                for k in range(DCH):
                    sTX[gp % 2].inc(yq.dma_start_transpose(
                        bigA[:, k, gp * 128:(gp + 1) * 128],
                        xtm[:, gp % 2, k * 128:(k + 1) * 128]), 16)

            @blk.vector
            def _(v):
                v.drain()
                s_mx.inc(v.tensor_reduce(m1arr[:, TC - 1:TC],
                                         xtm[:, (TC - 1) % 2, :], AX.X, ALU.max,
                                         apply_absolute_value=True))

        def tq_done(tt, c):
            return 16 * 32 * (tt // 2) + 16 * 8 * (c + 1)

        def kp_ap(g):
            return (kptm if (g % 4) < 2 else kpt2)[:, g % 2, :]

        def vt_ap(g):
            return (vtm if (g % 4) < 2 else vt2)[:, g % 2, :]

        def emitB(tt):
            @blk.tensor
            def _(e, tt=tt):
                if tt == 0:
                    e.wait_ge(s_wqc.h, 25)
                    e.wait_ge(s_cr.h, CR_P)
                e.wait_ge(sTX[0].h, 128 * (2 * tt + 2))
                e.wait_ge(sTX[1].h, 128 * (2 * tt + 2))
                for jc in range(16):
                    bank = pb[jc % 2]
                    if tt * 16 + jc >= 2:
                        e.wait_ge(s_qe.h, tt * 16 + jc - 1)
                    for k in range(DCH):
                        mm = e.matmul(bank[:], qkvT[:, k, jc * 128:(jc + 1) * 128],
                                      bigA[:, k, tt * 512:(tt + 1) * 512],
                                      start=(k == 0), stop=(k == DCH - 1))
                    s_qm.inc(mm)

            @blk.scalar
            def _(a, tt=tt):
                for jc in range(16):
                    a.wait_ge(s_qm.h, tt * 16 + jc + 1)
                    if tt >= 1 and jc == 0:
                        a.wait_ge(s_cm.h, 16 * tt)
                    s_qe.inc(a.activation(qfm[:, jc, :], pb[jc % 2][:], AF.Identity,
                                          bias=qkb[:, jc:jc + 1]))

            for c in range(4):
                g = tt * 4 + c

                @blk.tensor
                def _(e, tt=tt, c=c, g=g):
                    for half in range(2):
                        vidx = 2 * g + half
                        if vidx >= 1:
                            e.wait_ge(s_ve.h, vidx)
                        for k in range(DCH):
                            e.matmul(pb[2][:], bigA[:, k, g * 128:(g + 1) * 128],
                                     qkvT[:, k, 2048 + half * 512:2048 + (half + 1) * 512],
                                     start=(k == 0), stop=False)
                        mm = e.matmul(pb[2][:], ones_r[0:1, 0:128],
                                      qvb[0:1, half * 512:(half + 1) * 512],
                                      start=False, stop=True)
                        s_vm.inc(mm)
                    if c == 0:
                        e.wait_ge(s_qe.h, tt * 16 + 16)
                    for grp in range(4):
                        bank = pb[3 + grp % 2]
                        gidx = 4 * g + grp
                        if gidx >= 2:
                            e.wait_ge(s_cr.h, CR_P + gidx - 1)
                        for p4 in range(4):
                            p = (grp % 2) * 4 + p4
                            jj = (8 if grp >= 2 else 0) + p
                            mm = e.matmul(bank[:, p4 * 128:(p4 + 1) * 128],
                                          qfm[:, jj, c * 128:(c + 1) * 128],
                                          projBD[:], start=True, stop=True)
                        s_cm.inc(mm)

                @blk.scalar
                def _(a, tt=tt, c=c, g=g):
                    for half in range(2):
                        vidx = 2 * g + half
                        a.wait_ge(s_vm.h, vidx + 1)
                        if g >= 4 and half == 0:
                            a.wait_ge(s_kv.h, g - 3)
                        s_ve.inc(a.activation(vt_ap(g)[:, half * 512:(half + 1) * 512],
                                              pb[2][:], AF.Identity))
                    for grp in range(4):
                        gidx = 4 * g + grp
                        a.wait_ge(s_cr.h, CR_P + gidx + 1)
                        if gidx >= 2:
                            a.wait_ge(s_ep.h, gidx - 1)
                        s_ee.inc(a.activation(estg[:, grp % 2, :], esub[:, grp % 2, :],
                                              AF.Exp, bias=cstL8[:]))

                @blk.vector
                def _(v, tt=tt, c=c, g=g):
                    for grp in range(4):
                        bank = pb[3 + grp % 2]
                        gidx = 4 * g + grp
                        v.wait_ge(s_cm.h, gidx + 1)
                        bv = bank[:].rearrange("p (h m) -> p h m", m=64)
                        v.tensor_reduce(maxg[:, grp % 2, :].unsqueeze(2), bv,
                                        AX.X, ALU.max)
                        v.drain()
                        if gidx >= 2:
                            v.wait_ge(s_ee.h, gidx - 1)
                        s_cr.inc(v.tensor_tensor(
                            esub[:, grp % 2, :].rearrange("p (h m) -> p h m", m=64), bv,
                            maxg[:, grp % 2, :].unsqueeze(2).broadcast_to([128, 8, 64]),
                            ALU.subtract))
                        v.wait_ge(s_ee.h, gidx + 1)
                        half = grp % 2
                        if grp < 2:
                            if tt >= 1:
                                v.wait_ge(sTQ[(tt - 1) % 2].h, tq_done(tt - 1, c))
                            dst = qptm[:, g % 4, half * 512:(half + 1) * 512]
                        else:
                            if g >= 4:
                                v.wait_ge(s_kv.h, g - 3)
                            dst = kp_ap(g)[:, half * 512:(half + 1) * 512]
                        s_ep.inc(v.tensor_scalar(dst, estg[:, grp % 2, :], epsc[:],
                                                 None, ALU.add))

                @blk.sync
                def _(yq, tt=tt, c=c, g=g):
                    yq.wait_ge(sTQ[tt % 2].h, sTQ[tt % 2].n)
                    yq.wait_ge(s_ep.h, 4 * g + 2)
                    if tt >= 2 and c == 0:
                        yq.wait_ge(sQW[tt % 2].h, 16 * ((tt - 2) // 2 + 1))
                    for p in range(NP):
                        sTQ[tt % 2].inc(yq.dma_start_transpose(
                            qproll[:, tt % 2, p, c * 128:(c + 1) * 128],
                            qptm[:, g % 4, p * 128:(p + 1) * 128]), 16)

            @blk.tensor
            def _(e, tt=tt):
                for c in range(4):
                    g = tt * 4 + c
                    e.wait_ge(s_ep.h, 4 * g + 4)
                    e.wait_ge(s_ve.h, 2 * g + 2)
                    for p in range(NP):
                        bank = pb[5 + p // 4]
                        e.matmul(bank[:, (p % 4) * 128:(p % 4 + 1) * 128],
                                 kp_ap(g)[:, p * 128:(p + 1) * 128],
                                 vt_ap(g)[:, p * 128:(p + 1) * 128],
                                 start=(g == 0), stop=(g == TC - 1),
                                 skip_group_check=True)
                    for p in range(NP):
                        mm = e.matmul(pb[7][:, p:p + 1],
                                      kp_ap(g)[:, p * 128:(p + 1) * 128],
                                      ones_cb[:], start=(g == 0), stop=(g == TC - 1),
                                      skip_group_check=True)
                    s_kv.inc(mm)

            @blk.gpsimd
            def _(g_, tt=tt):
                g_.wait_ge(sTQ[tt % 2].h, tq_done(tt, 3))
                sQW[tt % 2].inc(g_.dma_start(out=sp_qp[:, :, tt * 512:(tt + 1) * 512],
                                             in_=qproll[:, tt % 2, :, :]), 16)

        for tt in range(TT):
            emitA(tt)
            if tt > 0:
                emitB(tt - 1)
        emitA_tail()
        emitB(TT - 1)
        TQ_TOT = [sTQ[0].n, sTQ[1].n]

        @blk.scalar
        def _(a):
            a.wait_ge(s_kv.h, TC)
            a.copy(kv_acc[:, 0:512], pb[5][:])
            a.copy(kv_acc[:, 512:1024], pb[6][:])
            a.copy(kv_acc[:, 1024:1032], pb[7][:, 0:8])
            s_kve.inc(a.drain())

        @blk.vector
        def _(v):
            v.wait_ge(s_mx.h, TC)
            v.tensor_reduce(pm24[:, 0:1], m1arr[:], AX.X, ALU.max)
            s_bar.inc(v.drain())
        B1 = s_bar.n

        @blk.gpsimd
        def _(g):
            g.wait_ge(s_bar.h, B1)
            s_ccd.inc(g.dma_start(out=cc1_in.unsqueeze(1), in_=pm24[:, 0:1]), 16)
            g.wait_ge(s_ccd.h, s_ccd.n)
            s_cc.inc(g.collective_compute(
                "AllReduce", ALU.max, replica_groups=[list(range(n_cores))],
                ins=[cc1_in.opt()], outs=[cc1_out.opt()]))
            g.wait_ge(s_cc.h, s_cc.n)
            s_ccd.inc(g.dma_start(out=mrow[:, 0, :], in_=cc1_out.unsqueeze(0)), 16)
        CC1_D = s_ccd.n

        @blk.vector
        def _(v):
            v.wait_ge(s_ccd.h, CC1_D)
            v.tensor_reduce(msc[:, 0:1].unsqueeze(2), mrow[:, 0:1, :], AX.X, ALU.max)
            v.drain()
            v.tensor_scalar(msc[:, 0:1], msc[:, 0:1], 1.0 / QMAX, 1e-12,
                            ALU.mult, ALU.max)
            v.drain()
            v.reciprocal(msc[:, 1:2], msc[:, 0:1])
            s_bar.inc(v.drain())
        B1b = s_bar.n

        @blk.sync
        def _(yq):
            yq.wait_ge(s_bar.h, B1b)
            s_bars.inc(yq.dma_start(out=scr1.unsqueeze(0), in_=msc[0:1, 0:2]), 16)
        B1c = s_bars.n

        @blk.gpsimd
        def _(g):
            g.wait_ge(s_bars.h, B1c)
            s_gb.inc(g.dma_start(out=sc1[:].unsqueeze(1),
                                 in_=scr1.unsqueeze(0).partition_broadcast(128)), 16)
        GB1 = s_gb.n

        @blk.vector
        def _(v):
            v.wait_ge(s_gb.h, GB1)
            v.tensor_scalar(xscb[:], rstd_s[:], sc1[:, 1:2], None, ALU.mult)
            v.tensor_scalar(xbib[:], nmr_s[:], sc1[:, 1:2], MAGIC,
                            ALU.mult, ALU.add)
            s_st.inc(v.drain())
        XSC_ST = s_st.n

        groups = [list(range(n_cores // 2)), list(range(n_cores // 2, n_cores))]

        @blk.gpsimd
        def _(g):
            g.wait_ge(s_ccd.h, s_ccd.n)
            g.wait_ge(s_kve.h, 1)
            s_ccd.inc(g.dma_start(out=cc2_in, in_=kv_acc[:]), 16)
            g.wait_ge(s_ccd.h, s_ccd.n)
            s_cc.inc(g.collective_compute(
                "AllReduce", ALU.add, replica_groups=groups,
                ins=[cc2_in.opt()], outs=[cc2_out.opt()]))
            g.wait_ge(s_cc.h, s_cc.n)
            s_ccd.inc(g.dma_start(out=kv_acc[:], in_=cc2_out), 16)
        CC2_D = s_ccd.n

        @blk.gpsimd
        def _(g):
            g.wait_ge(s_gb.h, s_gb.n)
            g.wait_ge(s_kv.h, TC)
            g.wait_ge(sTQ[0].h, TQ_TOT[0])
            g.wait_ge(sTQ[1].h, TQ_TOT[1])
            s_gb.inc(g.dma_start(out=selP[:].rearrange("q p c -> q (p c)"), in_=selc), 16)
        SEL_D = s_gb.n

        for g in range(TC):
            @blk.sync
            def _(yq, g=g):
                if g == 0:
                    yq.wait_ge(s_ap.h, TC)
                yq.wait_ge(s_a2.h, max(0, g - 1))
                sXA[g % 2].inc(yq.dma_start(out=xf[:, g % 2, :],
                                            in_=x[g * 128:(g + 1) * 128, :]), 16)

            @blk.scalar
            def _(a, g=g):
                a.wait_ge(sXA[g % 2].h, 16 * (TC // 2 + g // 2 + 1))
                if g == 0:
                    a.wait_ge(s_st.h, XSC_ST)
                if g >= 2:
                    a.wait_ge(s_v2.h, g - 1)
                s_a2.inc(a.activation(xnf[:, g % 2, :], xf[:, g % 2, :],
                                      AF.Identity, bias=xbib[:, g:g + 1],
                                      scale=xscb[:, g:g + 1]))

            @blk.vector
            def _(v, g=g):
                v.wait_ge(s_a2.h, g + 1)
                if g < 2:
                    v.wait_ge(s_mx.h, TC)
                    v.wait_ge(sTX[g % 2].h, 128 * (TC // 2))
                else:
                    v.wait_ge(sTX[g % 2].h, 128 * (TC // 2) + 128 * (g // 2))
                s_v2.inc(v.tensor_scalar(xtm[:, g % 2, :], xnf[:, g % 2, :],
                                         MAGIC, sc1[:, 0:1], ALU.subtract,
                                         ALU.mult))

            @blk.sync
            def _(yq, g=g):
                p = g % 2
                yq.wait_ge(sTX[p].h, sTX[p].n)
                yq.wait_ge(s_kv.h, 4 * (g // 4 + 1))
                yq.wait_ge(s_v2.h, g + 1)
                for k in range(DCH):
                    sTX[p].inc(yq.dma_start_transpose(
                        bigA[:, k, g * 128:(g + 1) * 128],
                        xtm[:, g % 2, k * 128:(k + 1) * 128]), 16)

        @blk.vector
        def _(v):
            v.wait_ge(s_ccd.h, CC2_D)
            v.wait_ge(s_gb.h, SEL_D)
            v.memset(kvbd[:], 0.0)
            v.memset(normW[:], 0.0)
            v.drain()
            for p in range(NP):
                v.tensor_copy(kvbd[0:64, p, 0:64], kv_acc[0:64, p * 128:p * 128 + 64])
                v.tensor_copy(kvbd[64:128, p, 64:128],
                              kv_acc[64:128, p * 128 + 64:p * 128 + 128])
                v.tensor_copy(normW[0:64, p, 2 * p:2 * p + 1],
                              kv_acc[0:64, 1024 + p:1025 + p])
                v.tensor_copy(normW[64:128, p, 2 * p + 1:2 * p + 2],
                              kv_acc[64:128, 1024 + p:1025 + p])
            s_dv.inc(v.drain())
        DV0 = s_dv.n

        for tt in range(TT):
            @blk.sync
            def _(yq, tt=tt):
                if tt == 0:
                    yq.wait_ge(sQW[0].h, 16 * ((TT + 1) // 2))
                    yq.wait_ge(sQW[1].h, 16 * (TT // 2))
                if tt >= 2:
                    yq.wait_ge(s_dm.h, 17 * (tt - 1))
                sQR[tt % 2].inc(yq.dma_start(out=qproll[:, tt % 2, :, :],
                                             in_=sp_qp[:, :, tt * 512:(tt + 1) * 512]), 16)

            Bd = 17 * tt
            Bv = DV0 + 18 * tt

            @blk.tensor
            def _(e, tt=tt, Bd=Bd, Bv=Bv):
                e.wait_ge(sQR[tt % 2].h, 16 * (tt // 2 + 1))
                if tt == 0:
                    e.wait_ge(s_dv.h, DV0)
                else:
                    e.wait_ge(s_dv.h, DV0 + 18 * (tt - 1) + 2)
                for p in range(NP):
                    mm = e.matmul(pb[0][0:16, :], normW[:, p, :], qproll[:, tt % 2, p, :],
                                  start=(p == 0), stop=(p == NP - 1))
                s_dm.inc(mm)
                for p in range(NP):
                    ob = pb[1 + p % 2]
                    bb = pb[3 + p % 2]
                    if p == 0:
                        e.wait_ge(s_dv.h, Bv + 2)
                        if tt >= 1:
                            e.wait_ge(s_dv.h, DV0 + 18 * tt)
                            e.wait_ge(s_de.h, 8 * tt)
                    if p >= 2:
                        e.wait_ge(s_dv.h, Bv + 2 * p)
                    mm = e.matmul(bb[:], selP[:, p, :], rec_sb[:], start=True, stop=True)
                    s_dm.inc(mm)
                    if p >= 2:
                        e.wait_ge(s_de.h, 8 * tt + p - 1)
                    e.wait_ge(s_dv.h, Bv + 4 + 2 * p)
                    mm = e.matmul(ob[:], kvbd[:, p, :], qsc[:], start=True, stop=True)
                    s_dm.inc(mm)

            @blk.vector
            def _(v, tt=tt, Bd=Bd, Bv=Bv):
                v.wait_ge(s_dm.h, Bd + 1)
                if tt >= 1:
                    v.wait_ge(s_dm.h, 17 * (tt - 1) + 16)
                v.tensor_scalar(nrm_sb[:], pb[0][0:16, :], ATT_EPS, None, ALU.add)
                v.drain()
                v.reciprocal(rec_sb[:], nrm_sb[:])
                s_dv.inc(v.drain(), 2)
                for p in range(NP):
                    v.wait_ge(s_dm.h, Bd + 2 + 2 * p)
                    bb = pb[3 + p % 2]
                    s_dv.inc(v.tensor_tensor(qsc[:], qproll[:, tt % 2, p, :], bb[:],
                                             ALU.mult), 2)

            @blk.scalar
            def _(a, tt=tt, Bd=Bd):
                for p in range(NP):
                    a.wait_ge(s_dm.h, Bd + 3 + 2 * p)
                    ob = pb[1 + p % 2]
                    if tt >= 2 and p == 0:
                        a.wait_ge(sOW[tt % 2].h, 16 * ((tt - 2) // 2 + 1))
                    s_de.inc(a.activation(qfm[:, (tt % 2) * 8 + p, :], ob[:],
                                          AF.Identity))

            @blk.gpsimd
            def _(g_, tt=tt):
                g_.wait_ge(s_de.h, 8 * (tt + 1))
                sOW[tt % 2].inc(g_.dma_start(out=sp_oe[:, :, tt * 512:(tt + 1) * 512],
                                             in_=qfm[:, (tt % 2) * 8:(tt % 2) * 8 + 8, :]), 16)
        D_DM = s_dm.n
        D_DE = s_de.n
        DV_TOT = s_dv.n
        OW_TOT = [sOW[0].n, sOW[1].n]

        def emit_dequant(i, src_ap_fn, wt_dst_fn, first_g_wait=None):
            b = i % 2

            @blk.gpsimd
            def _(g_, i=i, b=b):
                if first_g_wait is not None:
                    first_g_wait(g_, i)
                if i >= 2:
                    g_.wait_ge(s_dq2.h, DQ20 + (i - 2) + 1)
                sWL[b].inc(g_.dma_start(out=wnat[:, b, :], in_=src_ap_fn(i)), 16)
            wl_n = sWL[b].n

            @blk.vector
            def _(v, i=i, b=b, wl_n=wl_n):
                v.wait_ge(sWL[b].h, wl_n)
                if i >= 2:
                    v.wait_ge(s_dq2.h, DQ20 + (i - 2) + 1)
                fv = wnat[:, b, :].rearrange("p (a c) -> p a c", c=32)
                v.tensor_reduce(wsc[:, b, :].unsqueeze(2), fv, AX.X, ALU.max,
                                apply_absolute_value=True)
                v.drain()
                v.tensor_scalar(wsc[:, b, :], wsc[:, b, :], 1.0 / 7.0, 1e-12,
                                ALU.mult, ALU.max)
                v.drain()
                v.reciprocal(wrc[:, b, :], wsc[:, b, :])
                v.drain()
                s_dqv.inc(v.tensor_tensor(
                    fv, fv, wrc[:, b, :].unsqueeze(2).broadcast_to([128, 32, 32]),
                    ALU.mult))

            @blk.scalar
            def _(a, i=i, b=b):
                a.wait_ge(s_dqv.h, DQV0 + i + 1)
                s_dqs.inc(a.activation(wnat[:, b, :], wnat[:, b, :], AF.Identity,
                                       bias=cstM[:]))

            @blk.vector
            def _(v, i=i, b=b):
                v.wait_ge(s_dqs.h, DQS0 + i + 1)
                if i >= 2:
                    v.wait_ge(sWT[b].h, WT0[b] + 128 * ((i - 2) // 2 + 1))
                s_dq2.inc(v.scalar_tensor_tensor(
                    wbq[:, b, :].rearrange("p (a c) -> p a c", c=32),
                    wnat[:, b, :].rearrange("p (a c) -> p a c", c=32), -MAGIC,
                    wsc[:, b, :].unsqueeze(2).broadcast_to([128, 32, 32]),
                    ALU.add, ALU.mult))

            @blk.sync
            def _(yq, i=i, b=b):
                yq.wait_ge(sWT[b].h, sWT[b].n)
                yq.wait_ge(s_dq2.h, DQ20 + i + 1)
                for k in range(DCH):
                    sWT[b].inc(yq.dma_start_transpose(
                        wt_dst_fn(i, k), wbq[:, b, k * 128:(k + 1) * 128]), 16)

        DQV0 = s_dqv.n
        DQS0 = s_dqs.n
        DQ20 = s_dq2.n
        WT0 = [sWT[0].n, sWT[1].n]
        base_fm_g = s_fm.n
        base_fe_g = s_fe.n
        base_fr_g = s_fr.n
        gate_fw = [sFW[0].n, sFW[1].n]

        def gate_rest(ic):
            @blk.tensor
            def _(e, ic=ic):
                e.wait_ge(sWT[ic % 2].h, WT0[ic % 2] + 128 * (ic // 2 + 1))
                if ic == 0:
                    e.wait_ge(s_kve.h, 1)
                for tt in range(TT):
                    q = ic * TT + tt
                    if ic == 0:
                        e.wait_ge(sTX[0].h, 128 * (TC // 2) + 128 * (2 * tt + 2))
                        e.wait_ge(sTX[1].h, 128 * (TC // 2) + 128 * (2 * tt + 2))
                    if q >= 2:
                        e.wait_ge(s_fe.h, base_fe_g + q - 1)
                    for k in range(DCH):
                        mm = e.matmul(pb[5 + tt % 2][:], gT[:, ic % 2, k, :],
                                      bigA[:, k, tt * 512:(tt + 1) * 512],
                                      start=(k == 0), stop=(k == DCH - 1))
                    s_fm.inc(mm)

            @blk.scalar
            def _(a, ic=ic):
                for tt in range(TT):
                    q = ic * TT + tt
                    a.wait_ge(s_fm.h, base_fm_g + q + 1)
                    if tt == 0 and ic >= 2:
                        a.wait_ge(sFW[ic % 2].h,
                                  gate_fw[ic % 2] + 16 * ((ic - 2) // 2 + 1))
                    s_fe.inc(a.activation(fst[:, ic % 2, tt * 512:(tt + 1) * 512],
                                          pb[5 + tt % 2][:], AF.Identity))

            @blk.vector
            def _(v, ic=ic):
                v.wait_ge(s_fe.h, base_fe_g + TT * (ic + 1))
                s_fr.inc(v.tensor_reduce(m3arr[:, ic:ic + 1], fst[:, ic % 2, :],
                                         AX.X, ALU.max, apply_absolute_value=True))

            @blk.gpsimd
            def _(g_, ic=ic):
                g_.wait_ge(s_fr.h, base_fr_g + ic + 1)
                sFW[ic % 2].inc(g_.dma_start(out=sp_g[ic], in_=fst[:, ic % 2, :]), 16)

        def g_wait0(g_, i):
            if i == 0:
                g_.wait_ge(s_kv.h, TC)
        for i in range(ICH + 1):
            if i < ICH:
                emit_dequant(i, (lambda j: w_gate[j * 128:(j + 1) * 128, :]),
                             (lambda j, k: gT[:, j % 2, k, :]),
                             first_g_wait=g_wait0)
            if i >= 1:
                gate_rest(i - 1)
        GATE_FR = s_fr.n

        def barrier(cc_in, cc_out, srcs, scr, dst, nvals, derive, wait_fn=None):
            @blk.vector
            def _(v):
                if wait_fn is not None:
                    wait_fn(v)
                for i, src in enumerate(srcs):
                    v.tensor_reduce(pm24[:, i:i + 1], src[:], AX.X, ALU.max)
                s_bar.inc(v.drain())
            bv = s_bar.n

            @blk.gpsimd
            def _(g):
                g.wait_ge(s_ccd.h, s_ccd.n)
                g.wait_ge(s_bar.h, bv)
                k = len(srcs)
                for i in range(k):
                    s_ccd.inc(g.dma_start(out=cc_in[i * 128:(i + 1) * 128].unsqueeze(1),
                                          in_=pm24[:, i:i + 1]), 16)
                g.wait_ge(s_ccd.h, s_ccd.n)
                s_cc.inc(g.collective_compute(
                    "AllReduce", ALU.max, replica_groups=[list(range(n_cores))],
                    ins=[cc_in.opt()], outs=[cc_out.opt()]))
                g.wait_ge(s_cc.h, s_cc.n)
                s_ccd.inc(g.dma_start(
                    out=mrow[:, 0:k, :],
                    in_=cc_out.rearrange("(k p) -> k p", p=128).unsqueeze(0)), 16)
            cd = s_ccd.n

            @blk.vector
            def _(v):
                v.wait_ge(s_ccd.h, cd)
                k = len(srcs)
                v.tensor_reduce(msc[:, 0:k].unsqueeze(2), mrow[:, 0:k, :], AX.X, ALU.max)
                v.drain()
                derive(v)
                s_bar.inc(v.drain())
            bv2 = s_bar.n

            @blk.sync
            def _(yq):
                yq.wait_ge(s_bars.h, s_bars.n)
                yq.wait_ge(s_bar.h, bv2)
                s_bars.inc(yq.dma_start(out=scr[0:nvals].unsqueeze(0), in_=msc[0:1, 0:nvals]), 16)
            bv3 = s_bars.n

            @blk.gpsimd
            def _(g):
                g.wait_ge(s_gb.h, s_gb.n)
                g.wait_ge(s_bars.h, bv3)
                s_gb.inc(g.dma_start(out=dst[:, 0:nvals].unsqueeze(1),
                                     in_=scr[0:nvals].unsqueeze(0).partition_broadcast(128)), 16)
            return s_gb.n

        def derive3(v):
            v.tensor_scalar(msc[:, 0:1], msc[:, 0:1], 1.0 / QMAX, 1e-12,
                            ALU.mult, ALU.max)
            v.drain()
            v.reciprocal(msc[:, 1:2], msc[:, 0:1])
        GB3 = barrier(cc3_in, cc3_out, [m3arr], scr3, sc3, 2, derive3,
                      wait_fn=lambda v: v.wait_ge(s_fr.h, GATE_FR))

        DQV0 = s_dqv.n
        DQS0 = s_dqs.n
        DQ20 = s_dq2.n
        WT0 = [sWT[0].n, sWT[1].n]
        base_fm_u = s_fm.n
        base_fe_u = s_fe.n
        base_fr_u = s_fr.n
        up_fw = [sFW[0].n, sFW[1].n]

        H2 = T // 2

        def up_mm(ic):
            @blk.tensor
            def _(e, ic=ic):
                e.wait_ge(sWT[ic % 2].h, WT0[ic % 2] + 128 * (ic // 2 + 1))
                for tt in range(TT):
                    q = ic * TT + tt
                    if q >= 2:
                        e.wait_ge(s_fe.h, base_fe_u + q - 1)
                    for k in range(DCH):
                        mm = e.matmul(pb[5 + tt % 2][:], gT[:, ic % 2, k, :],
                                      bigA[:, k, tt * 512:(tt + 1) * 512],
                                      start=(k == 0), stop=(k == DCH - 1))
                    s_fm.inc(mm)

        def up_rest_iii(ic):
            @blk.scalar
            def _(a, ic=ic):
                for tt in range(TT):
                    q = ic * TT + tt
                    a.wait_ge(s_fm.h, base_fm_u + q + 1)
                    if tt == 0:
                        if ic >= 2:
                            a.wait_ge(sFW[ic % 2].h,
                                      up_fw[ic % 2] + 16 * ((ic - 2) // 2 + 1))
                        else:
                            a.wait_ge(sFW[ic % 2].h, up_fw[ic % 2])
                    s_fe.inc(a.activation(fst[:, ic % 2, tt * 512:(tt + 1) * 512],
                                          pb[5 + tt % 2][:], AF.Identity))

            @blk.vector
            def _(v, ic=ic):
                v.wait_ge(s_fe.h, base_fe_u + TT * (ic + 1))
                s_fr.inc(v.tensor_reduce(m2arr[:, ic:ic + 1], fst[:, ic % 2, :],
                                         AX.X, ALU.max, apply_absolute_value=True))

            @blk.gpsimd
            def _(g_, ic=ic):
                g_.wait_ge(s_fr.h, base_fr_u + ic + 1)
                sFW[ic % 2].inc(g_.dma_start(out=sp_u[ic], in_=fst[:, ic % 2, :]), 16)

            @blk.sync
            def _(yq, ic=ic):
                yq.wait_ge(sFW[ic % 2].h, gate_fw[ic % 2] + 16 * (ic // 2 + 1))
                if ic >= 2:
                    yq.wait_ge(s_3s.h, 16 * (ic - 1))
                sR3[ic % 2].inc(yq.dma_start(out=gst[:, ic % 2, :], in_=sp_g[ic]), 16)

            for h in range(2):
                hc = 2 * ic + h
                H4 = T // 4

                @blk.scalar
                def _(a, ic=ic, h=h, hc=hc, H4=H4):
                    if h == 0:
                        a.wait_ge(sR3[ic % 2].h, 16 * (ic // 2 + 1))
                    if hc == 0:
                        a.wait_ge(s_gb.h, GB3)
                        a.wait_ge(s_dm.h, D_DM)
                        a.wait_ge(s_dv.h, DV_TOT)
                        a.wait_ge(s_de.h, D_DE)
                        a.wait_ge(sOW[0].h, OW_TOT[0])
                        a.wait_ge(sOW[1].h, OW_TOT[1])
                        a.wait_ge(s_a2.h, TC)
                        a.wait_ge(s_v2.h, TC)
                    elif hc >= 2:
                        a.wait_ge(s_3v.h, 4 * (hc - 2) + 2)
                    base = 8 * hc
                    src_g = gst[:, ic % 2, h * (T // 2):(h + 1) * (T // 2)]
                    for q in range(2):
                        s_3s.inc(a.activation(t1h[:, hc % 2, q * H4:(q + 1) * H4],
                                              src_g[:, q * H4:(q + 1) * H4],
                                              AF.Identity, bias=cstM[:], scale=sc3[:, 1:2]))
                    for q in range(2):
                        a.wait_ge(s_3s.h, base + 1 + q)
                        s_3s.inc(a.activation(g1h[:, hc % 2, q * H4:(q + 1) * H4],
                                              t1h[:, hc % 2, q * H4:(q + 1) * H4],
                                              AF.Identity, bias=cstMn[:]))
                    for q in range(2):
                        a.wait_ge(s_3s.h, base + 3 + q)
                        s_3s.inc(a.activation(t1h[:, hc % 2, q * H4:(q + 1) * H4],
                                              g1h[:, hc % 2, q * H4:(q + 1) * H4],
                                              AF.Sigmoid, scale=sc3[:, 0:1]))
                    for q in range(2):
                        a.wait_ge(s_3s.h, base + 5 + q)
                        s_3s.inc(a.activation(t1h[:, hc % 2, q * H4:(q + 1) * H4],
                                              t1h[:, hc % 2, q * H4:(q + 1) * H4],
                                              AF.Identity, bias=cstM[:], scale=65536.0))

                @blk.vector
                def _(v, ic=ic, h=h, hc=hc):
                    v.wait_ge(s_3s.h, 8 * hc + 8)
                    tv = t1h[:, hc % 2, :]
                    v.tensor_scalar(tv, tv, MAGIC, 65535.0, ALU.subtract, ALU.min)
                    v.drain()
                    if h == 0 and ic >= 2:
                        v.wait_ge(sGW[ic % 2].h, 16 * ((ic - 2) // 2 + 1))
                    s_3v.inc(v.tensor_tensor(
                        g2r[:, ic % 2, h * (T // 2):(h + 1) * (T // 2)],
                        g1h[:, hc % 2, :], tv, ALU.mult), 2)
                    v.drain()
                    s_3v.inc(v.tensor_reduce(
                        m4arr[:, hc:hc + 1],
                        g2r[:, ic % 2, h * (T // 2):(h + 1) * (T // 2)],
                        AX.X, ALU.max, apply_absolute_value=True), 2)

            @blk.gpsimd
            def _(g_, ic=ic):
                g_.wait_ge(s_3v.h, 4 * (2 * ic + 1) + 4)
                sGW[ic % 2].inc(g_.dma_start(out=sp_g2[ic], in_=g2r[:, ic % 2, :]), 16)

        for i in range(ICH + 1):
            if i < ICH:
                emit_dequant(i, (lambda j: w_up[j * 128:(j + 1) * 128, :]),
                             (lambda j, k: gT[:, j % 2, k, :]))
                up_mm(i)
            if i >= 1:
                up_rest_iii(i - 1)
        UP_FM = s_fm.n
        UP_FR = s_fr.n
        III_3V = s_3v.n
        III_3S = s_3s.n
        GW_TOT = [sGW[0].n, sGW[1].n]

        def derive4(v):
            v.tensor_scalar(msc[:, 0:1], msc[:, 0:1], 1.0 / QMAX, 1e-12,
                            ALU.mult, ALU.max)
            v.tensor_tensor(msc[:, 1:2], msc[:, 1:2], sc3[0:1, 0:1], ALU.mult)
            v.drain()
            v.reciprocal(msc[:, 2:3], msc[:, 0:1])
            v.tensor_scalar(msc[:, 1:2], msc[:, 1:2], 1.0 / (65536.0 * QMAX),
                            1e-12, ALU.mult, ALU.max)
            v.drain()
            v.reciprocal(msc[:, 3:4], msc[:, 1:2])
            v.tensor_tensor(msc[:, 4:5], msc[:, 1:2], msc[:, 0:1], ALU.mult)
            v.drain()
            v.tensor_tensor(msc[:, 3:4], msc[:, 3:4], sc3[0:1, 0:1], ALU.mult)
            v.drain()
            v.tensor_scalar(msc[:, 3:4], msc[:, 3:4], 1.0 / 65536.0, None,
                            ALU.mult)

        def wait4(v):
            v.wait_ge(s_fr.h, UP_FR)
            v.wait_ge(s_3v.h, III_3V)
        GB4 = barrier(cc4_in, cc4_out, [m2arr, m4arr], scr4, sc24, 5, derive4,
                      wait_fn=wait4)

        DQV0 = s_dqv.n
        DQS0 = s_dqs.n
        DQ20 = s_dq2.n
        WT0 = [sWT[0].n, sWT[1].n]
        base_r3b = [sR3[0].n, sR3[1].n]

        def g_wait_d(g_, i):
            if i == 0:
                g_.wait_ge(s_fm.h, UP_FM)

        def emit_downw(i):
            dc, t3 = i // 3, i % 3

            def dst_fn(j, k, t3=t3, dc=dc):
                return downT[:, t3 * 8 + k, dc * 128:(dc + 1) * 128]
            emit_dequant(i, (lambda j, dc=dc, t3=t3:
                             w_down[dc * 128:(dc + 1) * 128,
                                    t3 * 1024:(t3 + 1) * 1024]),
                         dst_fn, first_g_wait=g_wait_d)

        def emit_iva(ic):
            @blk.sync
            def _(yq, ic=ic):
                yq.wait_ge(sFW[ic % 2].h, up_fw[ic % 2] + 16 * (ic // 2 + 1))
                if ic < 2:
                    yq.wait_ge(s_3s.h, III_3S)
                else:
                    yq.wait_ge(s_4s.h, 6 * (ic - 2) + 2)
                sR3[ic % 2].inc(yq.dma_start(out=gst[:, ic % 2, :], in_=sp_u[ic]), 16)
                yq.wait_ge(sGW[ic % 2].h, GW_TOT[ic % 2])
                if ic >= 2:
                    yq.wait_ge(sORW[ic % 2].h, 16 * ((ic - 2) // 2 + 1))
                sG2R[ic % 2].inc(yq.dma_start(out=g2r[:, ic % 2, :], in_=sp_g2[ic]), 16)

            @blk.scalar
            def _(a, ic=ic):
                a.wait_ge(sR3[ic % 2].h, base_r3b[ic % 2] + 16 * (ic // 2 + 1))
                if ic == 0:
                    a.wait_ge(s_gb.h, GB4)
                    a.wait_ge(s_3s.h, III_3S)
                    a.wait_ge(s_3v.h, III_3V)
                elif ic >= 2:
                    a.wait_ge(s_4v.h, 3 * (ic - 2) + 1)
                base = 6 * ic
                uv = iva_u[:, ic % 2, :]
                for h in range(2):
                    s_4s.inc(a.activation(uv[:, h * H2:(h + 1) * H2],
                                          gst[:, ic % 2, h * H2:(h + 1) * H2],
                                          AF.Identity, bias=cstM[:], scale=sc24[:, 2:3]))
                a.wait_ge(sG2R[ic % 2].h, 16 * (ic // 2 + 1))
                for h in range(2):
                    s_4s.inc(a.activation(g2r[:, ic % 2, h * H2:(h + 1) * H2],
                                          g2r[:, ic % 2, h * H2:(h + 1) * H2],
                                          AF.Identity, bias=cstM[:], scale=sc24[:, 3:4]),
                             3 if h == 1 else 1)

            @blk.vector
            def _(v, ic=ic):
                v.wait_ge(s_4s.h, 6 * ic + 6)
                v.tensor_scalar(iva_u[:, ic % 2, :], iva_u[:, ic % 2, :],
                                MAGIC, None, ALU.subtract)
                v.drain()
                s_4v.inc(v.scalar_tensor_tensor(g2r[:, ic % 2, :], g2r[:, ic % 2, :],
                                                -MAGIC, iva_u[:, ic % 2, :],
                                                ALU.add, ALU.mult))
                v.drain()
                s_4v.inc(v.tensor_reduce(m5arr[:, ic:ic + 1], g2r[:, ic % 2, :],
                                         AX.X, ALU.max, apply_absolute_value=True), 2)

            @blk.gpsimd
            def _(g_, ic=ic):
                g_.wait_ge(s_4v.h, 3 * ic + 3)
                sORW[ic % 2].inc(g_.dma_start(out=sp_or[ic], in_=g2r[:, ic % 2, :]), 16)

        for i in range(ICH + 1):
            if i < ICH:
                emit_downw(i)
            if i >= 1:
                emit_iva(i - 1)
        IVA_4V = s_4v.n
        IVA_4S = s_4s.n
        DWT_N = [sWT[0].n, sWT[1].n]
        FW_TOT = [sFW[0].n, sFW[1].n]
        ORW_TOT = [sORW[0].n, sORW[1].n]

        @blk.gpsimd
        def _(g):
            for c in range(DCH):
                if c == 0:
                    g.wait_ge(sFW[0].h, FW_TOT[0])
                    g.wait_ge(sFW[1].h, FW_TOT[1])
                if c >= 2:
                    g.wait_ge(s_pj.h, c - 1)
                s_pjl.inc(g.dma_start(out=pjst[:, c % 2, :],
                                      in_=proj_wT[c * 128:(c + 1) * 128, :]), 16)

        @blk.scalar
        def _(a):
            for c in range(DCH):
                a.wait_ge(s_pjl.h, 32 * (c // 2 + 1))
                if c == 0:
                    a.wait_ge(s_fm.h, UP_FM)
                s_pj.inc(a.copy(projT[:, c, :], pjst[:, c % 2, :]))
        PJ_N = s_pj.n

        def derive5(v):
            v.tensor_tensor(msc[:, 0:1], msc[:, 0:1], sc24[0:1, 4:5], ALU.mult)
            v.drain()
            v.tensor_scalar(msc[:, 0:1], msc[:, 0:1], 1.0 / QMAX, 1e-12,
                            ALU.mult, ALU.max)
            v.drain()
            v.reciprocal(msc[:, 1:2], msc[:, 0:1])
            v.drain()
            v.tensor_tensor(msc[:, 1:2], msc[:, 1:2], sc24[0:1, 4:5], ALU.mult)
        GB5 = barrier(cc5_in, cc5_out, [m5arr], scr5, sc5, 2, derive5,
                      wait_fn=lambda v: v.wait_ge(s_4v.h, IVA_4V))

        for tt in range(TT):
            @blk.sync
            def _(yq, tt=tt):
                if tt == 0:
                    yq.wait_ge(sORW[0].h, ORW_TOT[0])
                    yq.wait_ge(sORW[1].h, ORW_TOT[1])
                    yq.wait_ge(sWT[0].h, DWT_N[0])
                    yq.wait_ge(sWT[1].h, DWT_N[1])
                    yq.wait_ge(s_pj.h, PJ_N)
                else:
                    yq.wait_ge(s_5v.h, tt)
                s_ord.inc(yq.dma_start(
                    out=orst[:],
                    in_=sp_or.rearrange("i p t -> p i t")[:, :, tt * 512:(tt + 1) * 512]), 16)

            @blk.scalar
            def _(a, tt=tt):
                a.wait_ge(s_ord.h, 16 * (tt + 1))
                if tt == 0:
                    a.wait_ge(s_gb.h, GB5)
                ov = orst[:].rearrange("p a b -> p (a b)")
                s_5s.inc(a.activation(ov, ov, AF.Identity, bias=cstM[:],
                                      scale=sc5[:, 1:2]))

            @blk.vector
            def _(v, tt=tt):
                v.wait_ge(s_5s.h, tt + 1)
                if tt == 0:
                    v.wait_ge(s_4s.h, IVA_4S)
                    v.wait_ge(s_4v.h, IVA_4V)
                if tt >= 2:
                    v.wait_ge(s_tf.h, 4 * (tt - 2) + 4)
                s_5v.inc(v.tensor_scalar(
                    oTt[:, tt % 2, :, :].rearrange("p a b -> p (a b)"),
                    orst[:].rearrange("p a b -> p (a b)"),
                    MAGIC, sc5[:, 0:1], ALU.subtract, ALU.mult))

            for c in range(4):
                fb = 4 * tt + c

                @blk.sync
                def _(yq, fb=fb):
                    if fb >= 2:
                        yq.wait_ge(s_vf.h, fb - 1)
                        yq.wait_ge(s_tf.h, fb - 1)
                    sRF[fb % 2].inc(yq.dma_start(out=oest[:, fb % 2, :, :],
                                                 in_=sp_oe[:, :, fb * 128:(fb + 1) * 128]), 16)
                    sRF[fb % 2].inc(yq.dma_start(out=xrs[:, fb % 2, :],
                                                 in_=x[fb * 128:(fb + 1) * 128, :]), 16)

                @blk.tensor
                def _(e, tt=tt, c=c, fb=fb):
                    if fb == 0:
                        e.wait_ge(sWT[0].h, DWT_N[0])
                        e.wait_ge(sWT[1].h, DWT_N[1])
                        e.wait_ge(s_pj.h, PJ_N)
                    e.wait_ge(s_5v.h, tt + 1)
                    e.wait_ge(sRF[fb % 2].h, 32 * (fb // 2 + 1))
                    if fb >= 2:
                        e.wait_ge(s_vf.h, fb - 1)
                    bk0 = pb[(fb % 2) * 2]
                    bk1 = pb[(fb % 2) * 2 + 1]
                    for ic in range(ICH):
                        e.matmul(bk0[:], oTt[:, tt % 2, ic, c * 128:(c + 1) * 128],
                                 downT[:, ic, 0:512], start=(ic == 0), stop=False)
                        e.matmul(bk1[:], oTt[:, tt % 2, ic, c * 128:(c + 1) * 128],
                                 downT[:, ic, 512:1024], start=(ic == 0), stop=False)
                    for kc in range(DCH):
                        e.matmul(bk0[:], oest[:, fb % 2, kc, :], projT[:, kc, 0:512],
                                 start=False, stop=False)
                        e.matmul(bk1[:], oest[:, fb % 2, kc, :], projT[:, kc, 512:1024],
                                 start=False, stop=False)
                    e.matmul(bk0[:], ones_r[0:1, 0:128], prb[0:1, 0:512],
                             start=False, stop=True)
                    mm = e.matmul(bk1[:], ones_r[0:1, 0:128], prb[0:1, 512:1024],
                                  start=False, stop=True)
                    s_tf.inc(mm)

                @blk.vector
                def _(v, fb=fb):
                    v.wait_ge(s_tf.h, fb + 1)
                    if fb >= 2:
                        v.wait_ge(sOUT[fb % 2].h, 16 * ((fb - 2) // 2 + 1))
                    bk0 = pb[(fb % 2) * 2]
                    bk1 = pb[(fb % 2) * 2 + 1]
                    v.tensor_tensor(osb[:, fb % 2, 0:512], bk0[:], xrs[:, fb % 2, 0:512],
                                    ALU.add)
                    v.tensor_tensor(osb[:, fb % 2, 512:1024], bk1[:],
                                    xrs[:, fb % 2, 512:1024], ALU.add)
                    s_vf.inc(v.drain())

                @blk.scalar
                def _(a, fb=fb):
                    a.wait_ge(s_vf.h, fb + 1)
                    sOUT[fb % 2].inc(a.dma_start(out=y[fb * 128:(fb + 1) * 128, :],
                                                 in_=osb[:, fb % 2, :]), 16)

        @blk.gpsimd
        def _(g):
            g.wait_ge(sOUT[0].h, sOUT[0].n)
            g.wait_ge(sOUT[1].h, sOUT[1].n)

    return nc


from concourse.bass_utils import run_bass_kernel_spmd

_T = 4096
_NCORES = 8
_CACHE = {}


def _sel_const():
    sel = np.zeros((16, NP, 128), np.float32)
    for p in range(NP):
        sel[2 * p, p, 0:64] = 1.0
        sel[2 * p + 1, p, 64:128] = 1.0
    return np.ascontiguousarray(sel.reshape(16, NP * 128))


def kernel(L0_attn_x, L0_attn_qkv_w, L0_attn_qkv_b, L0_attn_proj_w, L0_attn_proj_b,
           L0_attn_projection_matrix, L0_ffn_w_gate, L0_ffn_w_up, L0_ffn_w_down):
    x = np.ascontiguousarray(np.asarray(L0_attn_x, dtype=np.float32).reshape(-1, D))
    assert x.shape[0] == _T * _NCORES
    qkv_wT = np.ascontiguousarray(np.asarray(L0_attn_qkv_w, dtype=np.float32).T)
    qkv_b = np.ascontiguousarray(np.asarray(L0_attn_qkv_b, dtype=np.float32))
    proj_wT = np.ascontiguousarray(np.asarray(L0_attn_proj_w, dtype=np.float32).T)
    proj_b = np.ascontiguousarray(np.asarray(L0_attn_proj_b, dtype=np.float32))
    pmatT = np.ascontiguousarray(np.asarray(L0_attn_projection_matrix, dtype=np.float32).T)
    qkbh = np.ascontiguousarray(qkv_b[0:2048].reshape(16, 128).T)
    w_gate = np.ascontiguousarray(np.asarray(L0_ffn_w_gate, dtype=np.float32))
    w_up = np.ascontiguousarray(np.asarray(L0_ffn_w_up, dtype=np.float32))
    w_down = np.ascontiguousarray(np.asarray(L0_ffn_w_down, dtype=np.float32))
    selc = _sel_const()

    if _T not in _CACHE:
        _CACHE[_T] = build(_T, n_cores=_NCORES)
    nc = _CACHE[_T]

    in_maps = []
    for c in range(_NCORES):
        in_maps.append({
            "x": x[c * _T:(c + 1) * _T],
            "qkv_wT": qkv_wT, "qkv_b": qkv_b,
            "proj_wT": proj_wT, "proj_b": proj_b,
            "pmatT": pmatT, "qkbh": qkbh,
            "w_gate": w_gate, "w_up": w_up, "w_down": w_down,
            "selc": selc,
        })
    res = run_bass_kernel_spmd(nc, in_maps, core_ids=list(range(_NCORES)))
    y = np.concatenate([np.asarray(r["y"]) for r in res.results], axis=0)
    return y.reshape(2, 16384, 1024)
